# version 03 (final)
# baseline (speedup 1.0000x reference)
import math
from contextlib import ExitStack

import numpy as np
import ml_dtypes

import concourse.bass as bass
import concourse.mybir as mybir
from concourse.bass_utils import run_bass_kernel_spmd

F32 = mybir.dt.float32
BF16 = mybir.dt.bfloat16
AF = mybir.ActivationFunctionType
ALU = mybir.AluOpType
AX = mybir.AxisListType

BF = ml_dtypes.bfloat16

N_CORES = 8
SEQS = 2
T = 2048
D = 1024
NT = 16
EPS = 1e-5

ENGINES = ("tensor", "vector", "scalar", "gpsimd", "sync")


class Prog:
    def __init__(self, nc):
        self.nc = nc
        self.q = {e: [] for e in ENGINES}
        self.sem_h = {}
        self.cnt = {}
        self.hwm = {e: {} for e in ENGINES}
        self.stack = ExitStack()
        for s in ("pe", "ve", "sc", "po", "sdg", "sds"):
            self.sem(s)

    def sem(self, name):
        if name not in self.sem_h:
            self.sem_h[name] = self.stack.enter_context(self.nc.semaphore(name))
            self.cnt[name] = 0
        return self.sem_h[name]

    def cur(self, name):
        return self.cnt[name]

    _AUTO_SER = {"vector": "ve", "scalar": "sc", "gpsimd": "po"}

    def emit(self, eng, fn, waits=(), inc=None, n=1):
        own = self._AUTO_SER.get(eng)
        if own is not None and fn is not None and inc == own:
            waits = list(waits) + [(own, self.cnt[own])]
        pruned = []
        hw = self.hwm[eng]
        for s, v in waits:
            if s in ("sdg", "sds") and eng not in ("gpsimd", "sync"):
                v = self.cnt[s]
            if v > 0 and hw.get(s, 0) < v:
                hw[s] = v
                pruned.append((self.sem_h[s], v))
        tok = None
        if inc is not None:
            self.cnt[inc] += n
            tok = self.cnt[inc]
            self.q[eng].append((fn, pruned, (self.sem_h[inc], n)))
        else:
            self.q[eng].append((fn, pruned, None))
        return tok

    def barrier(self, engines=ENGINES, sems=None):
        if sems is None:
            sems = list(self.cnt.keys())
        for e in engines:
            hw = self.hwm[e]
            for s in sems:
                v = self.cnt.get(s, 0)
                if v > 0 and hw.get(s, 0) < v:
                    hw[s] = v
                    self.q[e].append((None, [(self.sem_h[s], v)], None))

    def replay(self, block):
        def mk(ops):
            def run(eng):
                for fn, waits, inc in ops:
                    for h, v in waits:
                        eng.wait_ge(h, v)
                    if fn is None:
                        continue
                    ins = fn(eng)
                    if inc is not None:
                        h, n = inc
                        ins.then_inc(h, n)
            return run
        for e in ENGINES:
            getattr(block, e)(mk(self.q[e]))


def _ap(x):
    return x[:] if isinstance(x, (bass.SBTensorHandle, bass.PSumTensorHandle)) \
        else x


def dma(P, queue, out, in_, waits=(), sem=None):
    if sem is None:
        sem = "sdg" if queue == "gpsimd" else "sds"
    P.sem(sem)
    if sem in ("sdg", "sds"):
        waits = list(waits) + [(sem, P.cnt[sem])]
    return P.emit(queue,
                  lambda e, o=_ap(out), i=_ap(in_): e.dma_start(o, i),
                  waits, inc=sem, n=16)


def mm(P, out, lhsT, rhs, start, stop, waits=(), inc="pe", tile_position=None):
    return P.emit(
        "tensor",
        lambda e, o=_ap(out), l=_ap(lhsT), r=_ap(rhs), st=start, sp=stop,
        tp=tile_position: e.matmul(o, l, r, start=st, stop=sp,
                                   tile_position=tp),
        waits, inc="pe")


def ve(P, name, *args, waits=(), inc="ve", **kw):
    return P.emit("vector",
                  lambda e, a=tuple(map(_ap, args)), k=kw:
                  getattr(e, name)(*a, **k),
                  waits, inc="ve")


def sc(P, name, *args, waits=(), inc="sc", **kw):
    return P.emit("scalar",
                  lambda e, a=tuple(map(_ap, args)), k=kw:
                  getattr(e, name)(*a, **k),
                  waits, inc="sc")


def po(P, name, *args, waits=(), inc="po", **kw):
    return P.emit("gpsimd",
                  lambda e, a=tuple(map(_ap, args)), k=kw:
                  getattr(e, name)(*a, **k),
                  waits, inc="po")


def tr(P, out_psum, in_sbuf, ident, waits=(), inc="pe"):
    return P.emit(
        "tensor",
        lambda e, o=_ap(out_psum), i=_ap(in_sbuf), d=_ap(ident):
        e.transpose(o, i, d),
        waits, inc="pe")


_NAME_CTR = [0]


def nm(base):
    _NAME_CTR[0] += 1
    return f"{base}_{_NAME_CTR[0]}"


def _bf(x):
    return np.ascontiguousarray(np.asarray(x), dtype=BF)


def _f32(x):
    return np.ascontiguousarray(np.asarray(x), dtype=np.float32)


def lhsT_layout(w_in_by_out):
    return _f32(w_in_by_out)


def fm_layout(v):
    v = np.asarray(v).reshape(-1)
    c = v.shape[0] // 128
    return _f32(v.reshape(c, 128).T)


class Env:
    def __init__(self, nc, P, aps):
        self.nc = nc
        self.P = P
        self.aps = aps
        self.sb = {}
        self._scr = {}
        self._ccol = {}

    def scratch(self, name, shape, dtype):
        if name not in self._scr:
            self._scr[name] = self.nc.dram_tensor(name, list(shape), dtype,
                                                  kind="Internal")
        return self._scr[name]

    def constcol(self, value):
        key = float(value)
        if key not in self._ccol:
            i = len(self._ccol)
            apv = self.sb["cc"][:, i:i + 1]
            self._ccol[key] = apv
            ve(self.P, "memset", apv, key, inc="ve")
        return self._ccol[key]


def emit_load_w(P, env, name, dst, kchunks, dout, dtype_note="",
                src_ap=None):
    if not isinstance(dst, bass.AP):
        dst = dst[:]
    base = src_ap if src_ap is not None else env.aps[name]
    src = base.rearrange("(c p) n -> p c n", p=128)
    return dma(P, "gpsimd", dst.rearrange("p (c n) -> p c n", n=dout), src)


def emit_ln_stats(P, res_ap, sq, stat, col, rms=False, double_ln=False,
                  eps=EPS, waits=()):
    ve(P, "tensor_tensor", sq, res_ap, res_ap, ALU.mult, waits=waits)
    ve(P, "reduce_sum", stat[:, col + 2:col + 3], sq, axis=AX.X)
    if not rms:
        ve(P, "reduce_sum", stat[:, col + 3:col + 4], res_ap, axis=AX.X)
        ve(P, "tensor_scalar_mul", stat[:, col + 3:col + 4],
           stat[:, col + 3:col + 4], 1.0 / D)
        ve(P, "tensor_tensor", stat[:, col + 4:col + 5],
           stat[:, col + 3:col + 4], stat[:, col + 3:col + 4], ALU.mult)
        s1 = (1.0 + eps) if double_ln else 1.0
        ee = (eps + eps * eps) if double_ln else eps
        ve(P, "tensor_scalar", stat[:, col + 2:col + 3],
           stat[:, col + 2:col + 3], s1 / D, None, ALU.mult)
        ve(P, "tensor_scalar", stat[:, col + 4:col + 5],
           stat[:, col + 4:col + 5], s1, None, ALU.mult)
        ve(P, "tensor_sub", stat[:, col + 2:col + 3],
           stat[:, col + 2:col + 3], stat[:, col + 4:col + 5])
        ve(P, "tensor_scalar", stat[:, col + 2:col + 3],
           stat[:, col + 2:col + 3], ee, None, ALU.add)
        sc(P, "activation", stat[:, col + 2:col + 3],
           stat[:, col + 2:col + 3], AF.Sqrt,
           waits=[("ve", P.cur("ve"))])
        t = ve(P, "reciprocal", stat[:, col:col + 1], stat[:, col + 2:col + 3],
               waits=[("sc", P.cur("sc"))])
        ve(P, "tensor_tensor", stat[:, col + 1:col + 2],
           stat[:, col + 3:col + 4], stat[:, col:col + 1], ALU.mult)
        t = ve(P, "tensor_scalar_mul", stat[:, col + 1:col + 2],
               stat[:, col + 1:col + 2], -1.0, inc="ve")
    else:
        ve(P, "tensor_scalar", stat[:, col + 2:col + 3],
           stat[:, col + 2:col + 3], 1.0 / D, eps, ALU.mult, ALU.add)
        sc(P, "activation", stat[:, col + 2:col + 3],
           stat[:, col + 2:col + 3], AF.Sqrt,
           waits=[("ve", P.cur("ve"))])
        t = ve(P, "reciprocal", stat[:, col:col + 1], stat[:, col + 2:col + 3],
               waits=[("sc", P.cur("sc"))])
        t = ve(P, "memset", stat[:, col + 1:col + 2], 0.0, inc="ve")
    return t


def emit_ln_apply(P, res_ap, stat, col, xn_out, waits=()):
    w = list(waits) + [("ve", P.cur("ve"))]
    return ve(P, "tensor_scalar", xn_out, res_ap, stat[:, col:col + 1],
              stat[:, col + 1:col + 2], ALU.mult, ALU.add, waits=w)


def cview(tsr, nch, stride, off, width):
    return (tsr[:, 0:nch * stride]
            .rearrange("p (c n) -> p c n", n=stride)[:, :, off:off + width])


def emit_transpose_tile(P, xn_tile, dst, ptp, ident, wait_ve_tok, nch=8,
                        gcol=None, bcol=None, wait_extra=()):
    for k in range(nch):
        w = list(wait_extra)
        if k == 0:
            w.append(("ve", wait_ve_tok))
        tr(P, ptp[:, k * 128:(k + 1) * 128],
           xn_tile[:, k * 128:(k + 1) * 128], ident, waits=w,
           inc="pe" if k == nch - 1 else None)
    ptok = P.cur("pe")
    src = ptp[:, 0:nch * 128]
    if dst.ndim == 3:
        src = src.rearrange("p (c n) -> p c n", n=128)
    if gcol is not None:
        t = None
        wv0 = [("pe", ptok), ("ve", P.cur("ve"))]
        for k in range(nch):
            d = dst[:, k] if dst.ndim == 3 else dst[:, k * 128:(k + 1) * 128]
            sck = ptp[:, k * 128:(k + 1) * 128]
            if bcol is not None:
                t = ve(P, "tensor_scalar", d, sck, gcol[:, k:k + 1],
                       bcol[:, k:k + 1], ALU.mult, ALU.add, waits=wv0)
            else:
                t = ve(P, "tensor_scalar", d, sck, gcol[:, k:k + 1], None,
                       ALU.mult, waits=wv0)
    else:
        t = ve(P, "tensor_copy", dst, src, waits=[("pe", ptok)], inc="ve")
    return t


def emit_proj(P, psum_cols, xnT_tile, w_sb, dout, col0, ncols, kch=8,
              wait_first=(), inc=None, start=True, bias_row=None, ones1=None):
    last = None
    nmm = kch + (1 if bias_row is not None else 0)
    for k in range(kch):
        last = mm(P, psum_cols,
                  xnT_tile[:, k * 128:(k + 1) * 128],
                  w_sb[:, k * dout + col0: k * dout + col0 + ncols],
                  start=start and (k == 0),
                  stop=(k == kch - 1 and bias_row is None),
                  waits=wait_first if k == 0 else (),
                  inc=inc if (k == kch - 1 and bias_row is None) else None)
    if bias_row is not None:
        last = mm(P, psum_cols, ones1, bias_row[:, col0:col0 + ncols],
                  start=False, stop=True, inc=inc)
    return last


def prep_l2_attn(inp, c):
    c["l2a_wvT"] = _f32(np.ascontiguousarray(inp["L2_attn_wv"].T))
    c["l2a_wq"] = _f32(inp["L2_attn_wq"])
    c["l2a_wk"] = _f32(inp["L2_attn_wk"])


def emit_l2_attn(P, env):
    nc, aps = env.nc, env.aps
    res = env.sb["res"]
    ident = env.sb["ident"]
    scale = float(T) ** -0.5

    wvs_d = env.scratch("l2a_wvs_d", [8 * 128, 1024], BF16)
    val_d = env.scratch("l2a_val_d", [SEQS * 16, 128, 1024], BF16)
    att_d = env.scratch("l2a_att_d", [16, 128, 2048], BF16)

    with ExitStack() as ctx:
        e = ctx.enter_context
        xn = e(nc.sbuf_tensor("l2a_xn", [128, 1024], BF16))
        xnT = e(nc.sbuf_tensor("l2a_xnT", [128, 1024], BF16))
        sq = e(nc.sbuf_tensor("l2a_sq", [128, 2048], F32))
        stat = e(nc.sbuf_tensor("l2a_stat", [128, 8], F32))
        rs = e(nc.sbuf_tensor("l2a_rs", [128, 4], F32))
        ps = e(nc.psum_tensor("l2a_ps", [128, 2048], F32))
        ptp = e(nc.psum_tensor("l2a_ptp", [128, 1024], BF16))
        ps2 = e(nc.psum_tensor("l2a_ps2", [128, 1024], F32))
        P.barrier()

        with nc.sbuf_tensor("l2a_wvp", [128, 3 * 1024], F32) as wvp:
            wsrc = aps["l2a_wvT"].rearrange("(c p) n -> p c n", p=128)
            wvsb = wvp[:, 2048:3072].bitcast(BF16)
            ld_tok = [0, 0]
            use_tok = [0, 0]
            st_tok = 0
            for ch in range(8):
                p2 = ch % 2
                P.sem(f"l2awv{p2}")
                ld_tok[p2] = dma(P, "gpsimd",
                                 wvp[:, p2 * 1024:(p2 + 1) * 1024],
                                 wsrc[:, ch], waits=[("sc", use_tok[p2])],
                                 sem=f"l2awv{p2}")
                v = wvp[:, p2 * 1024:(p2 + 1) * 1024]
                use_tok[p2] = sc(P, "activation", sq[:, 0:1024], v, AF.Exp,
                                 waits=[(f"l2awv{p2}", ld_tok[p2]),
                                        ("ve", P.cur("ve"))], inc="sc")
                ve(P, "reduce_sum", rs[:, 0:1], sq[:, 0:1024], axis=AX.X,
                   waits=[("sc", use_tok[p2])])
                ve(P, "reciprocal", rs[:, 1:2], rs[:, 0:1])
                ve(P, "tensor_scalar_mul", rs[:, 1:2], rs[:, 1:2], 1.0 / D)
                wt = ve(P, "tensor_scalar", wvsb[:, 0:1024],
                        sq[:, 0:1024], rs[:, 1:2], None, ALU.mult,
                        waits=[("l2awst", st_tok), ("ve", P.cur("ve"))])
                P.sem("l2awst")
                st_tok = dma(P, "sync", wvs_d.ap()
                             .rearrange("(c p) n -> p c n", p=128)[:, ch],
                             wvsb[:, 0:1024], waits=[("ve", wt)],
                             sem="l2awst")
            P.barrier()

        for s in range(SEQS):
            t0 = s * NT
            with (nc.sbuf_tensor(nm("l2a_w3"), [128, 3 * 8 * 1024], BF16) as w3,
                  nc.sbuf_tensor(nm("l2a_qkT"), [128, 2 * 8 * 2048], BF16) as qkT,
                  nc.sbuf_tensor(nm("l2a_vst"), [128, 2 * 1024], BF16) as vst):
                wvs = w3[:, 0:8192]
                wq = w3[:, 8192:16384]
                wk = w3[:, 16384:24576]
                qT = qkT[:, 0:16384]
                kT = qkT[:, 16384:32768]
                dma(P, "gpsimd", wvs.rearrange("p (c n) -> p c n", n=1024),
                    wvs_d.ap().rearrange("(c p) n -> p c n", p=128))
                emit_load_w(P, env, "l2a_wq", wq, 8, 1024)
                wt2 = emit_load_w(P, env, "l2a_wk", wk, 8, 1024)
                vst_tok = [0, 0]
                for i in range(NT):
                    rtile = res[:, (t0 + i) * 1024:(t0 + i + 1) * 1024]
                    emit_ln_stats(P, rtile, sq[:, 0:1024], stat, 0)
                    lt = emit_ln_apply(P, rtile, stat, 0, xn[:])
                    tt = emit_transpose_tile(P, xn[:], xnT[:], ptp, ident, lt)
                    for g in range(2):
                        pv = emit_proj(P, ps2[:, g * 512:(g + 1) * 512],
                                       xnT, wvs, 1024, g * 512, 512,
                                       wait_first=[("ve", tt), ("sdg", wt2)],
                                       inc="pe" if g == 1 else None)
                    p2 = i % 2
                    P.sem(f"l2avst{p2}")
                    vw = ve(P, "tensor_copy", vst[:, p2 * 1024:(p2 + 1) * 1024],
                            ps2[:],
                            waits=[("pe", pv), (f"l2avst{p2}", vst_tok[p2])],
                            inc="ve")
                    vst_tok[p2] = dma(P, "sync", val_d.ap()[s * 16 + i],
                                      vst[:, p2 * 1024:(p2 + 1) * 1024],
                                      waits=[("ve", vw)], sem=f"l2avst{p2}")
                    for wi, (wmat, dstT) in enumerate(((wq, qT), (wk, kT))):
                        for g in range(2):
                            pv = emit_proj(
                                P, ps[:, wi * 1024 + g * 512:
                                      wi * 1024 + (g + 1) * 512], xnT, wmat,
                                1024, g * 512, 512,
                                inc="pe" if g == 1 else None)
                        ct = ve(P, "tensor_copy", xn[:],
                                ps[:, wi * 1024:(wi + 1) * 1024],
                                waits=[("pe", pv)], inc="ve")
                        emit_transpose_tile(
                            P, xn[:], cview(dstT, 8, 2048, i * 128, 128),
                            ptp, ident, ct)
                P.barrier()

                et_prev = 0
                vt_prev = 0
                ast = 0
                for mc in range(16):
                    for g4 in range(4):
                        for k in range(8):
                            mm(P, ps[:, g4 * 512:(g4 + 1) * 512],
                               kT[:, k * 2048 + mc * 128:
                                  k * 2048 + mc * 128 + 128],
                               qT[:, k * 2048 + g4 * 512:
                                  k * 2048 + (g4 + 1) * 512],
                               start=(k == 0), stop=(k == 7),
                               waits=([("sc", et_prev), ("ve", vt_prev)]
                                      if (g4 == 0 and k == 0) else ()),
                               inc="pe" if k == 7 else None)
                    mtok = P.cur("pe")
                    vt_prev = ve(P, "reduce_max", stat[:, 0:1], ps[:],
                                 axis=AX.X, waits=[("pe", mtok)], inc="ve")
                    ve(P, "tensor_scalar_mul", stat[:, 1:2],
                       stat[:, 0:1], -scale)
                    attb = sq[:].bitcast(BF16)[:, 0:2048]
                    P.sem("l2aast")
                    et_prev = sc(P, "activation", attb, ps[:],
                                 AF.Exp, stat[:, 1:2], scale,
                                 waits=[("ve", P.cur("ve")),
                                        ("l2aast", ast)], inc="sc")
                    ve(P, "reduce_sum", stat[:, 2:3], attb, axis=AX.X,
                       waits=[("sc", et_prev)])
                    ve(P, "reciprocal", stat[:, 3:4], stat[:, 2:3])
                    aw = ve(P, "tensor_scalar", attb, attb,
                            stat[:, 3:4], None, ALU.mult,
                            waits=[("ve", P.cur("ve"))])
                    ast = dma(P, "sync", att_d.ap()[mc], attb,
                              waits=[("ve", aw)], sem="l2aast")
                P.barrier()

            with (nc.sbuf_tensor(nm("l2a_val"), [128, 16 * 1024], BF16) as val,
                  nc.sbuf_tensor(nm("l2a_ani"), [128, 2 * 2048], BF16) as ani):
                dma(P, "gpsimd", val[:].rearrange("p (c n) -> p c n", n=1024),
                    val_d.ap()[s * 16:(s + 1) * 16].rearrange(
                        "c p n -> p c n"))
                P.barrier(engines=("tensor", "vector"), sems=("sdg",))
                done_tok = [0, 0]
                av_tok = [0, 0]
                an_tok = [0, 0]
                for ni in range(16):
                    par = ni % 2
                    P.sem(f"l2aani{par}")
                    an_tok[par] = dma(
                        P, "gpsimd",
                        ani[:, par * 2048:(par + 1) * 2048]
                        .rearrange("p (c n) -> p c n", n=128),
                        att_d.ap()[:, :, ni * 128:(ni + 1) * 128]
                        .rearrange("c p n -> p c n"),
                        waits=[("pe", av_tok[par])], sem=f"l2aani{par}")
                    pside = ps[:, par * 1024:(par + 1) * 1024]
                    for g in range(2):
                        for mc2 in range(16):
                            mm(P, pside[:, g * 512:(g + 1) * 512],
                               ani[:, par * 2048 + mc2 * 128:
                                   par * 2048 + mc2 * 128 + 128],
                               val[:, mc2 * 1024 + g * 512:
                                   mc2 * 1024 + (g + 1) * 512],
                               start=(mc2 == 0), stop=(mc2 == 15),
                               waits=([("ve", done_tok[par]),
                                       (f"l2aani{par}", an_tok[par])]
                                      if (g == 0 and mc2 == 0) else ()),
                               inc="pe" if (g == 1 and mc2 == 15) else None)
                    avt = P.cur("pe")
                    av_tok[par] = avt
                    rtile = res[:, (t0 + ni) * 1024:(t0 + ni + 1) * 1024]
                    vtile = val[:, ni * 1024:(ni + 1) * 1024]
                    ve(P, "scalar_tensor_tensor", sq[:, 0:1024], vtile, -1.0,
                       pside, ALU.mult, ALU.add, waits=[("pe", avt)])
                    dt2 = ve(P, "tensor_add", rtile, rtile, sq[:, 0:1024],
                             inc="ve")
                    done_tok[par] = dt2
                P.barrier()
        P.barrier()


def prep_l1_ffn(inp, c):
    w_in = np.asarray(inp["L1_ffn_W_in"])
    w_out = np.asarray(inp["L1_ffn_W_out"])
    b_in = np.asarray(inp["L1_ffn_b_in"])
    wiT = np.zeros((1024, 2048), np.float32)
    wiT[:, :2040] = w_in.T
    woT = np.zeros((2048, 1024), np.float32)
    woT[:2040, :] = w_out.T
    bi = np.zeros((1, 2048), np.float32)
    bi[0, :2040] = b_in
    c["l1f_WinT"] = _f32(wiT)
    c["l1f_WoutT"] = _f32(woT)
    c["l1f_bin"] = _f32(bi)


def emit_l1_ffn(P, env):
    nc, aps = env.nc, env.aps
    res = env.sb["res"]
    ident = env.sb["ident"]
    ones1 = env.sb["ones1"]
    W = 2048

    with ExitStack() as ctx:
        e = ctx.enter_context
        wi = e(nc.sbuf_tensor(nm("l1f_wi"), [128, 8 * W], BF16))
        wo = e(nc.sbuf_tensor(nm("l1f_wo"), [128, 16 * 1024], BF16))
        brow = e(nc.sbuf_tensor(nm("l1f_br"), [1, W], BF16))
        xn = e(nc.sbuf_tensor(nm("l1f_xn"), [128, 1024], BF16))
        xnT = e(nc.sbuf_tensor(nm("l1f_xnT"), [128, 1024], BF16))
        dec = e(nc.sbuf_tensor(nm("l1f_dec"), [128, W], F32))
        rch = e(nc.sbuf_tensor(nm("l1f_rch"), [128, W], F32))
        acts = e(nc.sbuf_tensor(nm("l1f_act"), [128, W], F32))
        msk = e(nc.sbuf_tensor(nm("l1f_msk"), [128, W], BF16))
        mskT = e(nc.sbuf_tensor(nm("l1f_mskT"), [128, W], BF16))
        sq = e(nc.sbuf_tensor(nm("l1f_sq"), [128, 1024], F32))
        stat = e(nc.sbuf_tensor(nm("l1f_st"), [128, 8], F32))
        ps = e(nc.psum_tensor(nm("l1f_ps"), [128, W], F32))
        ptp = e(nc.psum_tensor(nm("l1f_ptp"), [128, 1024], BF16))
        psy = e(nc.psum_tensor(nm("l1f_psy"), [128, 1024], F32))
        P.barrier()

        emit_load_w(P, env, "l1f_WinT", wi, 8, W)
        emit_load_w(P, env, "l1f_WoutT", wo, 16, 1024)
        wld = dma(P, "gpsimd", brow[:], aps["l1f_bin"])
        P.barrier(engines=("tensor",), sems=("sdg",))

        sil_tok = 0
        dec_tok = 0
        y_tok = 0
        mres_tok = 0
        mtr_tok = 0
        for i in range(2 * NT):
            rtile = res[:, i * 1024:(i + 1) * 1024]
            emit_ln_stats(P, rtile, sq[:], stat, 0)
            lt = emit_ln_apply(P, rtile, stat, 0, xn[:])
            tt = emit_transpose_tile(P, xn[:], xnT[:], ptp, ident, lt)
            for g in range(4):
                pv = emit_proj(P, ps[:, g * 512:(g + 1) * 512], xnT, wi, W,
                               g * 512, 512,
                               wait_first=([("ve", tt), ("sc", sil_tok)]
                                           if g == 0 else ()),
                               bias_row=brow, ones1=ones1)
            dec_tok = ve(P, "tensor_single_scalar", dec[:], ps[:], 0.0,
                         ALU.is_gt, waits=[("pe", pv)])
            sil_tok = sc(P, "activation", acts[:], ps[:], AF.Silu,
                         waits=[("pe", pv), ("ve", mres_tok)])
            ve(P, "memset", rch[:], 0.0)
            rt3 = rch[:, 0:2040].rearrange("p (t n) -> p t n", n=255)
            dt3 = dec[:, 0:2040].rearrange("p (t n) -> p t n", n=255)
            ve(P, "memset", rt3[:, :, 0:1], 1.0)
            for d in range(7):
                s = 1 << d
                b0 = (1 << d) - 1
                b1 = (1 << (d + 1)) - 1
                par = rt3[:, :, b0:b0 + s]
                dcv = dt3[:, :, b0:b0 + s]
                ch = (rt3[:, :, b1:b1 + 2 * s]
                      .rearrange("p t (k two) -> p t k two", two=2))
                right = ch[:, :, :, 1]
                left = ch[:, :, :, 0]
                ve(P, "tensor_mul", right, par, dcv)
                ve(P, "tensor_sub", left, par, right)
            mt = ve(P, "tensor_mul", msk[:], acts[:], rch[:],
                    waits=[("sc", sil_tok), ("pe", mtr_tok)])
            tta = emit_transpose_tile(P, msk[:, 0:1024], mskT[:, 0:1024],
                                      ptp, ident, mt)
            tt2 = emit_transpose_tile(P, msk[:, 1024:2048], mskT[:, 1024:2048],
                                      ptp, ident, tta)
            mtr_tok = P.cur("pe")
            for g in range(2):
                pv2 = emit_proj(P, psy[:, g * 512:(g + 1) * 512], mskT, wo,
                                1024, g * 512, 512, kch=16,
                                wait_first=([("ve", tt2), ("ve", y_tok)]
                                            if g == 0 else ()))
            ve(P, "scalar_tensor_tensor", rtile[:, 0:512], psy[:, 0:512],
               0.5, rtile[:, 0:512], ALU.mult, ALU.add,
               waits=[("pe", pv2)])
            y_tok = mres_tok = ve(P, "scalar_tensor_tensor", rtile[:, 512:1024],
                                  psy[:, 512:1024], 0.5, rtile[:, 512:1024],
                                  ALU.mult, ALU.add)
        P.barrier()


def prep_l0_attn(inp, c):
    c["l0a_WqT"] = _f32(np.ascontiguousarray(np.asarray(inp["L0_attn_w_q"]).T))
    c["l0a_WkT"] = _f32(np.ascontiguousarray(np.asarray(inp["L0_attn_w_k"]).T))
    c["l0a_WvT"] = _f32(np.ascontiguousarray(np.asarray(inp["L0_attn_w_v"]).T))
    c["l0a_WoT"] = _f32(np.ascontiguousarray(np.asarray(inp["L0_attn_w_o"]).T))
    if "C_l0a_cos" not in c:
        half = 32
        inv = 10000.0 ** (-(np.arange(half) * 2.0 / 64.0))
        t = np.arange(T, dtype=np.float64)
        ang = t[:, None] * inv[None, :]
        cs = np.cos(ang).astype(np.float32).reshape(16, 128, 32)
        sn = np.sin(ang).astype(np.float32).reshape(16, 128, 32)
        c["C_l0a_cos"] = _f32(np.transpose(cs, (1, 0, 2)).reshape(128, 16 * 32))
        c["C_l0a_sin"] = _f32(np.transpose(sn, (1, 0, 2)).reshape(128, 16 * 32))
        m = (np.arange(128)[:, None] <= np.arange(128)[None, :])
        c["C_l0a_causal"] = _bf(m.astype(np.float32))


def emit_l0_attn(P, env):
    nc, aps = env.nc, env.aps
    res = env.sb["res"]
    ident = env.sb["ident"]

    q_d = env.scratch("l0a_q_d", [SEQS * 16, 128, 1024], BF16)
    k_d = env.scratch("l0a_k_d", [SEQS * 16, 128, 1024], BF16)

    with ExitStack() as ctx:
        e = ctx.enter_context
        val = e(nc.sbuf_tensor(nm("l0a_val"), [128, 16 * 1024], BF16))
        ao = e(nc.sbuf_tensor(nm("l0a_ao"), [128, 16 * 1024], BF16))
        cosT = e(nc.sbuf_tensor(nm("l0a_cos"), [128, 512], F32))
        sinT = e(nc.sbuf_tensor(nm("l0a_sin"), [128, 512], F32))
        cmask = e(nc.sbuf_tensor(nm("l0a_cm"), [128, 128], BF16))
        onec = e(nc.sbuf_tensor(nm("l0a_1c"), [128, 1], BF16))
        xn = e(nc.sbuf_tensor(nm("l0a_xn"), [128, 1024], BF16))
        xnT = e(nc.sbuf_tensor(nm("l0a_xnT"), [128, 1024], BF16))
        qf = e(nc.sbuf_tensor(nm("l0a_qf"), [128, 1024], F32))
        qb = e(nc.sbuf_tensor(nm("l0a_qb"), [128, 4 * 1024], BF16))
        sq = e(nc.sbuf_tensor(nm("l0a_sq"), [128, 1024], F32))
        stat = e(nc.sbuf_tensor(nm("l0a_st"), [128, 8], F32))
        rec = e(nc.sbuf_tensor(nm("l0a_rec"), [128, 2], F32))
        P.barrier()
        ve(P, "memset", onec[:], 1.0)
        dma(P, "gpsimd", cosT[:], aps["C_l0a_cos"])
        dma(P, "gpsimd", sinT[:], aps["C_l0a_sin"])
        dma(P, "gpsimd", cmask[:], aps["C_l0a_causal"])

        cosv = cosT[:].rearrange("p (i n) -> p i n", n=32)
        sinv = sinT[:].rearrange("p (i n) -> p i n", n=32)

        st_tok = {"q": [0, 0], "k": [0, 0]}
        for s in range(SEQS):
            t0 = s * NT
            with (nc.sbuf_tensor(nm("l0a_w3"), [128, 3 * 8 * 1024], BF16)
                  as w3,
                  nc.psum_tensor(nm("l0a_pp"), [128, 1024], F32) as pp,
                  nc.psum_tensor(nm("l0a_ptp"), [128, 1024], BF16) as ptp):
                wq = w3[:, 0:8192]
                wk = w3[:, 8192:16384]
                wv = w3[:, 16384:24576]
                emit_load_w(P, env, "l0a_WqT", wq, 8, 1024)
                emit_load_w(P, env, "l0a_WkT", wk, 8, 1024)
                emit_load_w(P, env, "l0a_WvT", wv, 8, 1024)
                P.barrier(engines=("tensor",), sems=("sdg",))
                ev_tok = 0
                for i in range(NT):
                    rtile = res[:, (t0 + i) * 1024:(t0 + i + 1) * 1024]
                    emit_ln_stats(P, rtile, sq[:], stat, 0)
                    lt = emit_ln_apply(P, rtile, stat, 0, xn[:])
                    tt = emit_transpose_tile(P, xn[:], xnT[:], ptp, ident, lt)
                    for g in range(2):
                        pv = emit_proj(P, pp[:, g * 512:(g + 1) * 512],
                                       xnT, wv, 1024, g * 512, 512,
                                       wait_first=[("ve", tt),
                                                   ("ve", ev_tok)])
                    ve(P, "tensor_copy", val[:, i * 1024:(i + 1) * 1024],
                       pp[:], waits=[("pe", pv)])
                    par = i % 2
                    for wn, wmat, dd in (("q", wq, q_d), ("k", wk, k_d)):
                        slot = (2 if wn == "k" else 0) + par
                        for g in range(2):
                            pv = emit_proj(P, pp[:, g * 512:(g + 1) * 512],
                                           xnT, wmat, 1024, g * 512, 512,
                                           wait_first=[("ve", P.cur("ve"))])
                        ev_tok = ve(P, "tensor_copy", qf[:], pp[:],
                                    waits=[("pe", pv)])
                        dst = qb[:, slot * 1024:(slot + 1) * 1024]
                        sname = f"l0a{wn}{par}"
                        P.sem(sname)
                        qe = qf[:].rearrange("p (h two n) -> p h two n",
                                             two=2, n=32)[:, :, 0]
                        qo = qf[:].rearrange("p (h two n) -> p h two n",
                                             two=2, n=32)[:, :, 1]
                        de = dst.rearrange("p (h two n) -> p h two n",
                                           two=2, n=32)[:, :, 0]
                        do = dst.rearrange("p (h two n) -> p h two n",
                                           two=2, n=32)[:, :, 1]
                        cB = (cosv[:, i:i + 1, :]
                              .broadcast_to([128, 16, 32]))
                        sB = (sinv[:, i:i + 1, :]
                              .broadcast_to([128, 16, 32]))
                        w0 = [(sname, st_tok[wn][par])]
                        ve(P, "tensor_mul", sq[:, 0:512]
                           .rearrange("p (h n) -> p h n", n=32), qe, cB)
                        ve(P, "tensor_mul", sq[:, 512:1024]
                           .rearrange("p (h n) -> p h n", n=32), qo, sB)
                        ve(P, "tensor_sub", de,
                           sq[:, 0:512].rearrange("p (h n) -> p h n", n=32),
                           sq[:, 512:1024].rearrange("p (h n) -> p h n", n=32),
                           waits=w0)
                        ve(P, "tensor_mul", sq[:, 0:512]
                           .rearrange("p (h n) -> p h n", n=32), qe, sB)
                        ve(P, "tensor_mul", sq[:, 512:1024]
                           .rearrange("p (h n) -> p h n", n=32), qo, cB)
                        rt = ve(P, "tensor_add", do,
                                sq[:, 0:512].rearrange("p (h n) -> p h n",
                                                       n=32),
                                sq[:, 512:1024].rearrange("p (h n) -> p h n",
                                                          n=32),
                                waits=w0)
                        st_tok[wn][par] = dma(P, "sync", dd.ap()[t0 + i], dst,
                                              waits=[("ve", rt)], sem=sname)
                P.barrier()

            with (nc.sbuf_tensor(nm("l0a_hs"), [128, 2 * 1024], BF16) as hs,
                  nc.sbuf_tensor(nm("l0a_qT"), [64, 2 * 2048], BF16) as qkT,
                  nc.sbuf_tensor(nm("l0a_PT"), [128, 2048], BF16) as PT,
                  nc.sbuf_tensor(nm("l0a_pm"), [128, 128], BF16) as pm,
                  nc.psum_tensor(nm("l0a_sc2"), [128, 1024], F32) as psc,
                  nc.psum_tensor(nm("l0a_av"), [128, 2048], F32) as pav,
                  nc.psum_tensor(nm("l0a_pt2"), [64, 1024], BF16) as pt2):
                qT = qkT[:, 0:2048]
                kT = qkT[:, 2048:4096]
                P.sem("l0ahs")
                hs_tok = 0
                for h in range(16):
                    hs_tok = dma(P, "gpsimd", hs[:, 0:1024]
                                 .rearrange("p (c n) -> p c n", n=64),
                                 q_d.ap()[t0:t0 + 16, :, h * 64:(h + 1) * 64]
                                 .rearrange("c p n -> p c n"),
                                 waits=[("pe", P.cur("pe"))], sem="l0ahs")
                    hs_tok = dma(P, "gpsimd", hs[:, 1024:2048]
                                 .rearrange("p (c n) -> p c n", n=64),
                                 k_d.ap()[t0:t0 + 16, :, h * 64:(h + 1) * 64]
                                 .rearrange("c p n -> p c n"),
                                 sem="l0ahs")
                    for wv2, off in ((0, 0), (1, 1024)):
                        for wave in range(2):
                            for k in range(8):
                                til = wave * 8 + k
                                tr(P, pt2[:, k * 128:(k + 1) * 128],
                                   hs[:, off + til * 64: off + til * 64 + 64],
                                   ident,
                                   waits=([("l0ahs", hs_tok),
                                           ("ve", P.cur("ve"))]
                                          if k == 0 else ()))
                            ptk = P.cur("pe")
                            ve(P, "tensor_copy",
                               qkT[:, wv2 * 2048 + wave * 1024:
                                   wv2 * 2048 + (wave + 1) * 1024],
                               pt2[:], waits=[("pe", ptk)])
                    P.barrier(engines=("tensor",), sems=("ve",))
                    ex_tok = [0, 0]
                    pm_tok = 0
                    for kc in range(16):
                        for hf in range(2):
                            for g in range(2):
                                mm(P, psc[:, g * 512:(g + 1) * 512],
                                   kT[:, kc * 128:(kc + 1) * 128],
                                   qT[:, hf * 1024 + g * 512:
                                      hf * 1024 + (g + 1) * 512],
                                   start=True, stop=True,
                                   waits=[("sc", P.cur("sc"))])
                            ex_tok[hf] = sc(
                                P, "activation",
                                PT[:, hf * 1024:(hf + 1) * 1024], psc[:],
                                AF.Exp, 0.0, 0.125,
                                waits=[("pe", P.cur("pe"))])
                        for qt in range(16):
                            c0 = qt * 128
                            pslice = PT[:, qt * 128:qt * 128 + 128]
                            mm(P, pav[:, c0 + 64:c0 + 65], pslice, onec[:],
                               start=(kc == 0 and qt % 4 == 0),
                               stop=(kc == 15 and qt % 4 == 3 and qt != 15),
                               waits=[("sc", ex_tok[qt // 8])])
                            if kc < qt:
                                mm(P, pav[:, c0:c0 + 64], pslice,
                                   val[:, kc * 1024 + h * 64:
                                       kc * 1024 + (h + 1) * 64],
                                   start=False, stop=False)
                            elif kc == qt:
                                mk = ve(P, "tensor_mul", pm[:], pslice,
                                        cmask[:],
                                        waits=[("sc", ex_tok[qt // 8]),
                                               ("pe", pm_tok)])
                                pm_tok = mm(
                                    P, pav[:, c0:c0 + 64], pm[:],
                                    val[:, kc * 1024 + h * 64:
                                        kc * 1024 + (h + 1) * 64],
                                    start=False, stop=(qt == 15),
                                    waits=[("ve", mk)])
                    avt = P.cur("pe")
                    for qt in range(16):
                        c0 = qt * 128
                        ve(P, "reciprocal", rec[:, 0:1],
                           pav[:, c0 + 64:c0 + 65],
                           waits=[("pe", avt)])
                        ve(P, "tensor_scalar",
                           ao[:, qt * 1024 + h * 64:qt * 1024 + (h + 1) * 64],
                           pav[:, c0:c0 + 64], rec[:, 0:1], None, ALU.mult,
                           waits=[("ve", P.cur("ve"))])
                P.barrier()

            with (nc.sbuf_tensor(nm("l0a_wo"), [128, 8 * 1024], BF16) as wo,
                  nc.psum_tensor(nm("l0a_py"), [128, 1024], F32) as py,
                  nc.psum_tensor(nm("l0a_pt3"), [128, 1024], BF16) as pt3):
                emit_load_w(P, env, "l0a_WoT", wo, 8, 1024)
                P.barrier(engines=("tensor",), sems=("sdg",))
                y_tok = 0
                for i in range(NT):
                    att = ao[:, i * 1024:(i + 1) * 1024]
                    tt = emit_transpose_tile(P, att, xnT[:], pt3, ident,
                                             P.cur("ve"))
                    for g in range(2):
                        pv = emit_proj(P, py[:, g * 512:(g + 1) * 512],
                                       xnT, wo, 1024, g * 512, 512,
                                       wait_first=[("ve", tt),
                                                   ("ve", y_tok)])
                    rtile = res[:, (t0 + i) * 1024:(t0 + i + 1) * 1024]
                    ve(P, "tensor_add", rtile[:, 0:512], rtile[:, 0:512],
                       py[:, 0:512], waits=[("pe", pv)])
                    y_tok = ve(P, "tensor_add", rtile[:, 512:1024],
                               rtile[:, 512:1024], py[:, 512:1024])
                P.barrier()
        P.barrier()


def _l0f_pos(f, layout):
    f = np.asarray(f)
    B = f // 128
    A = f % 128
    j = A // 4
    q = A % 4
    if layout == "f":
        return f
    if layout == "g":
        return j * 128 + q * 32 + B
    if layout == "pi":
        return B * 128 + q * 32 + j
    raise ValueError(layout)


def _l0f_stage_blocks(weights, k, layout):
    DH = 4096
    s = 1 << k
    nb = DH // (2 * s)
    w = np.asarray(weights)[k].reshape(nb, s, 2, 2).astype(np.float32)
    rows = np.arange(DH)
    j = rows // (2 * s)
    r = rows % (2 * s)
    i2 = r % s
    isb = (r >= s).astype(np.int64)
    partner = rows + s * (1 - 2 * isb)
    diag = w[j, i2, isb, isb]
    off = w[j, i2, isb, 1 - isb]
    ro = _l0f_pos(rows, layout)
    co_d = ro
    co_o = _l0f_pos(partner, layout)
    S = np.zeros((32, 128, 128), np.float32)
    S[ro // 128, ro % 128, co_d % 128] = diag
    assert np.all(co_o // 128 == ro // 128)
    S[ro // 128, ro % 128, co_o % 128] += off
    return S


def prep_l0_ffn(inp, c):
    if "l0f_WexpT" in c:
        return
    Wexp = np.asarray(inp["L0_ffn_W_exp"])
    Wcon = np.asarray(inp["L0_ffn_W_con"])
    up_w = np.asarray(inp["L0_ffn_up_weights"])
    dn_w = np.asarray(inp["L0_ffn_down_weights"])
    f = np.arange(4096)
    gpos = _l0f_pos(f, "g")
    c["l0f_WexpT"] = _f32(np.ascontiguousarray(Wexp.T))
    WconG = np.empty_like(np.asarray(Wcon, dtype=np.float32))
    WconG[gpos] = Wcon
    c["l0f_WconG"] = _f32(WconG)
    c["l0f_bexp_fm"] = fm_layout(inp["L0_ffn_b_exp"])
    ubg = np.empty(4096, np.float32)
    ubg[gpos] = np.asarray(inp["L0_ffn_up_bias"])
    c["l0f_ub_g_fm"] = fm_layout(ubg)
    c["l0f_bcd"] = _f32(np.concatenate(
        [np.asarray(inp["L0_ffn_b_con"]).reshape(1, -1),
         np.asarray(inp["L0_ffn_down_bias"]).reshape(1, -1)], axis=1))
    for name, wts, kk, g in (("u1", up_w, range(0, 7), "f"),
                             ("u2", up_w, range(7, 12), "g"),
                             ("d1", dn_w, range(0, 7), "pi"),
                             ("d2", dn_w, range(7, 12), "g")):
        blocks = [_l0f_stage_blocks(wts, k, g) for k in kk]
        arr = [blocks[0]] + [np.ascontiguousarray(b.transpose(0, 2, 1))
                             for b in blocks[1:]]
        c[f"l0f_S{name}"] = _f32(np.stack(arr))


def _l0f_chain(P, env, name, nst, cs, pc, dst_fn, transpose_out, ident_f,
               ptf):
    nc, aps = env.nc, env.aps
    src = aps[name]
    for Bc in range(32):
        ld = dma(P, "gpsimd", cs[:, 0:nst * 128]
                 .rearrange("p (k n) -> p k n", n=128),
                 src[:, Bc].rearrange("k p n -> p k n"),
                 waits=[("pe", P.cur("pe")), ("ve", P.cur("ve"))])
        cur = cs[:, 0:128]
        for k in range(1, nst):
            mm(P, pc[:, 0:128], cs[:, k * 128:(k + 1) * 128], cur,
               start=True, stop=True,
               waits=[("sdg", ld), ("ve", P.cur("ve"))] if k == 1
               else [("ve", P.cur("ve"))])
            cur = cs[:, nst * 128:(nst + 1) * 128]
            ve(P, "tensor_copy", cur, pc[:, 0:128],
               waits=[("pe", P.cur("pe"))])
        if transpose_out:
            tr(P, ptf[:, 0:128], cur, ident_f, waits=[("ve", P.cur("ve"))])
            cur2 = cs[:, (nst + 1) * 128:(nst + 2) * 128]
            ve(P, "tensor_copy", cur2, ptf[:, 0:128],
               waits=[("pe", P.cur("pe"))])
            cur = cur2
        dst_fn(Bc, cur)


def emit_l0_ffn(P, env):
    nc, aps = env.nc, env.aps
    res = env.sb["res"]
    ident = env.sb["ident"]
    identf = env.sb["identf"]
    ones1 = env.sb["ones1"]

    we2_d = env.scratch("l0f_we2_d", [32, 1024, 128], BF16)
    wc2_d = env.scratch("l0f_wc2_d", [32, 128, 1024], BF16)
    h1_d = env.scratch("l0f_h1_d", [32, 128, 512], BF16)
    h2_d = env.scratch("l0f_h2_d", [32, 128, 512], BF16)
    folded = False

    with ExitStack() as ctx:
        e = ctx.enter_context
        m2u = e(nc.sbuf_tensor(nm("l0f_m2u"), [128, 32 * 128], BF16))
        m1d = e(nc.sbuf_tensor(nm("l0f_m1d"), [128, 32 * 128], BF16))
        b1f = e(nc.sbuf_tensor(nm("l0f_b1"), [128, 32], F32))
        ubf = e(nc.sbuf_tensor(nm("l0f_ub"), [128, 32], F32))
        bcr = e(nc.sbuf_tensor(nm("l0f_bc"), [1, 2048], BF16))
        xn = e(nc.sbuf_tensor(nm("l0f_xn"), [128, 1024], BF16))
        xnT = e(nc.sbuf_tensor(nm("l0f_xnT"), [128, 8 * 512], BF16))
        sq = e(nc.sbuf_tensor(nm("l0f_sq"), [128, 1024], F32))
        stat = e(nc.sbuf_tensor(nm("l0f_st"), [128, 8], F32))
        P.barrier()
        if not folded:
            bc2 = dma(P, "gpsimd", bcr[:], aps["l0f_bcd"])
            ve(P, "tensor_add", bcr[:, 0:1024], bcr[:, 0:1024],
               bcr[:, 1024:2048], waits=[("sdg", bc2)])
            dma(P, "gpsimd", ubf[:], aps["l0f_ub_g_fm"])

        if not folded:
            with (nc.sbuf_tensor(nm("l0f_cs"), [128, 10 * 128], F32) as cs,
                  nc.sbuf_tensor(nm("l0f_fw"), [128, 2 * 1024], F32) as fw,
                  nc.sbuf_tensor(nm("l0f_fs"), [128, 1024], BF16) as fstg,
                  nc.sbuf_tensor(nm("l0f_bx"), [128, 32], F32) as bxf,
                  nc.psum_tensor(nm("l0f_pc"), [128, 512], F32) as pc,
                  nc.psum_tensor(nm("l0f_ptf"), [128, 128], F32) as ptf,
                  nc.psum_tensor(nm("l0f_pf"), [128, 1024], F32) as pf):
                bxt = dma(P, "gpsimd", bxf[:], aps["l0f_bexp_fm"])
                wsrc = aps["l0f_WexpT"].rearrange("(c p) n -> p c n", p=128)
                st_tok = 0

                def fold_g1(Bc, cur):
                    nonlocal st_tok
                    wl = dma(P, "gpsimd", fw[:, 0:1024], wsrc[:, Bc],
                             waits=[("pe", P.cur("pe"))])
                    mm(P, pc[:, 0:1], cur, bxf[:, Bc:Bc + 1],
                       start=True, stop=True,
                       waits=[("sdg", wl), ("sdg", bxt),
                              ("ve", P.cur("ve"))])
                    ve(P, "tensor_copy", b1f[:, Bc:Bc + 1], pc[:, 0:1],
                       waits=[("pe", P.cur("pe"))])
                    for dsl in range(8):
                        mm(P, pf[:, dsl * 128:(dsl + 1) * 128],
                           fw[:, dsl * 128:(dsl + 1) * 128], cur,
                           start=(dsl in (0, 4)), stop=(dsl in (3, 7)),
                           waits=[("ve", P.cur("ve"))] if dsl == 0 else ())
                    pvt = ve(P, "tensor_copy", fstg[:, 0:1024], pf[:],
                             waits=[("pe", P.cur("pe")), ("sds", st_tok)])
                    st_tok = dma(P, "sync", we2_d.ap()[Bc]
                                 .rearrange("(s p) f -> p s f", p=128),
                                 fstg[:, 0:1024]
                                 .rearrange("p (s f) -> p s f", f=128),
                                 waits=[("ve", pvt)])
                _l0f_chain(P, env, "l0f_Su1", 7, cs, pc, fold_g1, True,
                           identf, ptf)
                def keep_m2u(Bc, cur):
                    ve(P, "tensor_copy", m2u[:, Bc * 128:(Bc + 1) * 128],
                       cur, waits=[("ve", P.cur("ve"))])
                _l0f_chain(P, env, "l0f_Su2", 5, cs, pc, keep_m2u, True,
                           identf, ptf)
                def keep_m1d(Bc, cur):
                    ve(P, "tensor_copy", m1d[:, Bc * 128:(Bc + 1) * 128],
                       cur, waits=[("ve", P.cur("ve"))])
                _l0f_chain(P, env, "l0f_Sd1", 7, cs, pc, keep_m1d, True,
                           identf, ptf)
                csrc = aps["l0f_WconG"].rearrange("(c p) n -> p c n", p=128)

                def fold_g2d(Bc, cur):
                    nonlocal st_tok
                    wl = dma(P, "gpsimd", fw[:, 0:1024], csrc[:, Bc],
                             waits=[("pe", P.cur("pe"))])
                    for hh in range(2):
                        mm(P, pf[:, hh * 512:(hh + 1) * 512], cur,
                           fw[:, hh * 512:(hh + 1) * 512],
                           start=True, stop=True,
                           waits=[("sdg", wl), ("ve", P.cur("ve"))]
                           if hh == 0 else ())
                    pvt = ve(P, "tensor_copy", fstg[:, 0:1024], pf[:],
                             waits=[("pe", P.cur("pe")), ("sds", st_tok)])
                    for qq in range(4):
                        st_tok = dma(P, "sync",
                                     wc2_d.ap()[:, qq * 32 + Bc, :],
                                     fstg[qq * 32:(qq + 1) * 32, 0:1024],
                                     waits=[("ve", pvt)])
                _l0f_chain(P, env, "l0f_Sd2", 5, cs, pc, fold_g2d, False,
                           identf, ptf)
            P.barrier()

        P.barrier()
        for grp in range(8):
            gt0 = grp * 4
            with nc.psum_tensor(nm("l0f_ptp"), [128, 1024], BF16) as ptp:
                for ti in range(4):
                    rtile = res[:, (gt0 + ti) * 1024:(gt0 + ti + 1) * 1024]
                    emit_ln_stats(P, rtile, sq[:], stat, 0)
                    lt = emit_ln_apply(P, rtile, stat, 0, xn[:])
                    emit_transpose_tile(
                        P, xn[:], cview(xnT, 8, 512, ti * 128, 128),
                        ptp, ident, lt)
            with (nc.sbuf_tensor(nm("l0f_wb"), [128, 2 * 1024], BF16) as wb,
                  nc.sbuf_tensor(nm("l0f_hst"), [128, 2 * 512], BF16) as hst,
                  nc.sbuf_tensor(nm("l0f_hd"), [128, 32 * 512], BF16) as hd):
                P.sem("l0fw0"); P.sem("l0fw1")
                P.sem("l0fh0"); P.sem("l0fh1")
                wtk = [0, 0]
                htk = [0, 0]
                with nc.psum_tensor(nm("l0f_pb"), [128, 1024], F32) as pb:
                    for c2 in range(32):
                        pr = c2 % 2
                        wtk[pr] = dma(P, "gpsimd",
                                      wb[:, pr * 1024:(pr + 1) * 1024]
                                      .rearrange("p (k n) -> p k n", n=128),
                                      we2_d.ap()[c2]
                                      .rearrange("(k p) n -> p k n", p=128),
                                      waits=[("pe", P.cur("pe"))],
                                      sem=f"l0fw{pr}")
                        for k in range(8):
                            mm(P, pb[:, pr * 512:(pr + 1) * 512],
                               wb[:, pr * 1024 + k * 128:
                                  pr * 1024 + k * 128 + 128],
                               xnT[:, k * 512:(k + 1) * 512],
                               start=(k == 0), stop=(k == 7),
                               waits=[(f"l0fw{pr}", wtk[pr]),
                                      ("ve", P.cur("ve"))] if k == 0 else ())
                        hv = ve(P, "tensor_scalar",
                                hst[:, pr * 512:(pr + 1) * 512],
                                pb[:, pr * 512:(pr + 1) * 512],
                                b1f[:, c2:c2 + 1], None, ALU.add,
                                waits=[("pe", P.cur("pe")),
                                       (f"l0fh{pr}", htk[pr])])
                        htk[pr] = dma(P, "sync", h1_d.ap()[c2],
                                      hst[:, pr * 512:(pr + 1) * 512],
                                      waits=[("ve", hv)], sem=f"l0fh{pr}")
                    P.barrier()
                    for j in range(32):
                        pr = j % 2
                        for qq in range(4):
                            wtk[pr] = dma(
                                P, "gpsimd",
                                wb[qq * 32:(qq + 1) * 32,
                                   pr * 1024:pr * 1024 + 512],
                                h1_d.ap()[:, 4 * j + qq, :],
                                waits=[("pe", P.cur("pe"))]
                                if qq == 0 else (),
                                sem=f"l0fw{pr}")
                        mm(P, pb[:, pr * 512:(pr + 1) * 512],
                           m2u[:, j * 128:(j + 1) * 128],
                           wb[:, pr * 1024:pr * 1024 + 512],
                           start=True, stop=True,
                           waits=[(f"l0fw{pr}", wtk[pr]),
                                  ("sc", P.cur("sc"))])
                        hv = sc(P, "activation",
                                hst[:, pr * 512:(pr + 1) * 512],
                                pb[:, pr * 512:(pr + 1) * 512], AF.Gelu,
                                ubf[:, j:j + 1], 1.0,
                                waits=[("pe", P.cur("pe")),
                                       (f"l0fh{pr}", htk[pr])])
                        htk[pr] = dma(P, "sync", h2_d.ap()[j],
                                      hst[:, pr * 512:(pr + 1) * 512],
                                      waits=[("sc", hv)], sem=f"l0fh{pr}")
                    P.barrier()
                    for Bc in range(32):
                        pr = Bc % 2
                        for qq in range(4):
                            wtk[pr] = dma(
                                P, "gpsimd",
                                wb[qq * 32:(qq + 1) * 32,
                                   pr * 1024:pr * 1024 + 512],
                                h2_d.ap()[:, qq * 32 + Bc, :],
                                waits=[("pe", P.cur("pe"))]
                                if qq == 0 else (),
                                sem=f"l0fw{pr}")
                        mm(P, pb[:, pr * 512:(pr + 1) * 512],
                           m1d[:, Bc * 128:(Bc + 1) * 128],
                           wb[:, pr * 1024:pr * 1024 + 512],
                           start=True, stop=True,
                           waits=[(f"l0fw{pr}", wtk[pr]),
                                  ("ve", P.cur("ve"))])
                        ve(P, "tensor_copy", hd[:, Bc * 512:(Bc + 1) * 512],
                           pb[:, pr * 512:(pr + 1) * 512],
                           waits=[("pe", P.cur("pe"))])
                    P.barrier()
                with nc.psum_tensor(nm("l0f_py"), [128, 4096], F32) as py:
                    for Bc in range(32):
                        pr = Bc % 2
                        wtk[pr] = dma(
                            P, "gpsimd",
                            wb[:, pr * 1024:(pr + 1) * 1024],
                            wc2_d.ap()[Bc],
                            waits=[("pe", P.cur("pe")),
                                   ("ve", P.cur("ve"))]
                            if Bc < 2 else [("pe", P.cur("pe"))],
                            sem=f"l0fw{pr}")
                        for st2 in range(4):
                            for hh in range(2):
                                mm(P, py[:, st2 * 1024 + hh * 512:
                                      st2 * 1024 + (hh + 1) * 512],
                                   hd[:, Bc * 512 + st2 * 128:
                                      Bc * 512 + st2 * 128 + 128],
                                   wb[:, pr * 1024 + hh * 512:
                                      pr * 1024 + (hh + 1) * 512],
                                   start=(Bc == 0), stop=False,
                                   waits=[(f"l0fw{pr}", wtk[pr])]
                                   if (st2 == 0 and hh == 0) else ())
                    for st2 in range(4):
                        for hh in range(2):
                            mm(P, py[:, st2 * 1024 + hh * 512:
                                  st2 * 1024 + (hh + 1) * 512],
                               ones1, bcr[:, hh * 512:(hh + 1) * 512],
                               start=False, stop=True)
                    ytok = P.cur("pe")
                    for st2 in range(4):
                        rtile = res[:, (gt0 + st2) * 1024:
                                    (gt0 + st2 + 1) * 1024]
                        ve(P, "scalar_tensor_tensor", rtile[:, 0:512],
                           py[:, st2 * 1024:st2 * 1024 + 512], 0.5,
                           rtile[:, 0:512], ALU.mult, ALU.add,
                           waits=[("pe", ytok)])
                        ve(P, "scalar_tensor_tensor", rtile[:, 512:1024],
                           py[:, st2 * 1024 + 512:st2 * 1024 + 1024],
                           0.5, rtile[:, 512:1024], ALU.mult, ALU.add)
                    P.barrier()
        P.barrier()


def prep_l2_ffn(inp, c):
    if "l2f_wqkv" in c:
        return
    c["l2f_wqkv"] = _f32(inp["L2_ffn_wqkv"])
    c["l2f_wo"] = _f32(inp["L2_ffn_wo"])
    w1 = np.asarray(inp["L2_ffn_w1"])
    b1 = np.asarray(inp["L2_ffn_b1"])
    w2 = np.asarray(inp["L2_ffn_w2"])
    HID = 2730
    w1p = np.zeros((2, 1024, 6144), np.float32)
    w1p[:, :, 0:HID] = w1[:, :, 0:HID]
    w1p[:, :, 3072:3072 + HID] = w1[:, :, HID:2 * HID]
    b1p = np.zeros((2, 1, 6144), np.float32)
    b1p[:, 0, 0:HID] = b1[:, 0:HID]
    b1p[:, 0, 3072:3072 + HID] = b1[:, HID:2 * HID]
    w2p = np.zeros((2, 3072, 1024), np.float32)
    w2p[:, 0:HID, :] = w2
    c["l2f_w1p"] = _f32(w1p)
    c["l2f_b1p"] = _f32(b1p)
    c["l2f_w2p"] = _f32(w2p)
    c["l2f_b2"] = _f32(np.asarray(inp["L2_ffn_b2"]).reshape(2, 1, 1024))
    c["l2f_lng"] = _f32(np.stack([fm_layout(g) for g in inp["L2_ffn_ln_g"]]))
    c["l2f_lnb"] = _f32(np.stack([fm_layout(b) for b in inp["L2_ffn_ln_b"]]))
    c["l2f_qs"] = _f32(np.asarray(inp["L2_ffn_q_scale"]).reshape(2, 1, 64))
    c["l2f_ks"] = _f32(np.asarray(inp["L2_ffn_k_scale"]).reshape(2, 1, 64))
    if "C_l2f_cq" not in c:
        DH = 64
        W = 512
        n = T
        half = np.arange(0, DH, 2, dtype=np.float64)
        inv = 1.0 / 10000.0 ** (half / DH)
        t = np.arange(n, dtype=np.float64)
        fr = t[:, None] * inv[None, :]
        base = (half + 0.4 * DH) / (1.4 * DH)
        power = (t - n // 2) / float(W // 2)
        xs = base[None, :] ** power[:, None]
        cq = (np.cos(fr) * xs * 8.0).astype(np.float32)
        sq = (np.sin(fr) * xs * 8.0).astype(np.float32)
        ck = (np.cos(fr) / xs).astype(np.float32)
        sk = (np.sin(fr) / xs).astype(np.float32)

        def tile(a):
            return _f32(a.reshape(16, 128, 32).transpose(1, 0, 2)
                        .reshape(128, 512))
        c["C_l2f_cq"] = tile(cq)
        c["C_l2f_sq"] = tile(sq)
        c["C_l2f_ck"] = tile(ck)
        c["C_l2f_sk"] = tile(sk)
        m = np.full((8, 128, 512), -1e30, np.float32)
        for ri, r in enumerate(range(-4, 4)):
            kp = np.arange(128)[:, None]
            qp = np.arange(512)[None, :]
            dij = qp - r * 128 - kp
            m[ri] = np.where((dij >= 0) & (dij <= 512), 80.0, -1e30)
        c["C_l2f_mask"] = _bf(m)


def emit_l2_ffn(P, env):
    nc, aps = env.nc, env.aps
    res = env.sb["res"]
    ident = env.sb["ident"]
    ones1 = env.sb["ones1"]
    sav_d = env.scratch("l2f_sav_d", [32, 128, 1024], BF16)

    with ExitStack() as ctx:
        e = ctx.enter_context
        cqt = e(nc.sbuf_tensor(nm("l2f_cq"), [128, 4 * 512], F32))
        msk = e(nc.sbuf_tensor(nm("l2f_msk"), [128, 8 * 512], BF16))
        qsb = e(nc.sbuf_tensor(nm("l2f_qsb"), [128, 2 * 64], F32))
        rowb = e(nc.sbuf_tensor(nm("l2f_rows"), [1, 6144 + 1024], BF16))
        qsr = e(nc.sbuf_tensor(nm("l2f_qsr"), [1, 128], BF16))
        lngb = e(nc.sbuf_tensor(nm("l2f_lngb"), [128, 16], F32))
        onec = e(nc.sbuf_tensor(nm("l2f_1c"), [128, 1], BF16))
        xn = e(nc.sbuf_tensor(nm("l2f_xn"), [128, 1024], BF16))
        sq2 = e(nc.sbuf_tensor(nm("l2f_sq"), [128, 1024], F32))
        stat = e(nc.sbuf_tensor(nm("l2f_st"), [128, 16], F32))
        P.barrier()
        for i2, n2 in enumerate(("C_l2f_cq", "C_l2f_sq", "C_l2f_ck",
                                 "C_l2f_sk")):
            dma(P, "gpsimd", cqt[:, i2 * 512:(i2 + 1) * 512], aps[n2])
        dma(P, "gpsimd", msk[:].rearrange("p (r n) -> p r n", n=512),
            aps["C_l2f_mask"].rearrange("r p n -> p r n"))
        ve(P, "memset", onec[:], 1.0)
        dma(P, "sync", sav_d.ap().rearrange("c p n -> p c n"),
            res[:].rearrange("p (c n) -> p c n", n=1024))
        P.barrier()
        for i in range(2 * NT):
            rtile = res[:, i * 1024:(i + 1) * 1024]
            emit_ln_stats(P, rtile, sq2[:], stat, 0)
            emit_ln_apply(P, rtile, stat, 0, rtile)
        P.barrier()

        for li in range(2):
            dma(P, "gpsimd", rowb[:, 0:6144], aps["l2f_b1p"][li])
            dma(P, "gpsimd", rowb[:, 6144:7168], aps["l2f_b2"][li])
            dma(P, "gpsimd", lngb[:, 0:8], aps["l2f_lng"][li])
            dma(P, "gpsimd", lngb[:, 8:16], aps["l2f_lnb"][li])
            with nc.psum_tensor(nm("l2f_pq"), [128, 128], F32) as pq:
                dma(P, "gpsimd", qsr[:, 0:64], aps["l2f_qs"][li])
                qtk = dma(P, "gpsimd", qsr[:, 64:128], aps["l2f_ks"][li])
                mm(P, pq[:, 0:128], ones1, qsr[:], start=True, stop=True,
                   waits=[("sdg", qtk), ("ve", P.cur("ve"))])
                ve(P, "tensor_copy", qsb[:], pq[:],
                   waits=[("pe", P.cur("pe"))])
            P.barrier()
            for s in range(SEQS):
                _l2f_attn(P, env, li, s, cqt, msk, qsb, onec, xn, sq2, stat)
                _l2f_mlp(P, env, li, s, rowb, lngb, xn, sq2, stat)
        P.barrier()
        with nc.sbuf_tensor(nm("l2f_sv"), [128, 2 * 1024], BF16) as sv:
            P.sem("l2fsv0"); P.sem("l2fsv1")
            svt = [0, 0]
            utk = [0, 0]
            for i in range(2 * NT):
                pr = i % 2
                svt[pr] = dma(P, "gpsimd", sv[:, pr * 1024:(pr + 1) * 1024],
                              sav_d.ap()[i], waits=[("ve", utk[pr])],
                              sem=f"l2fsv{pr}")
                rtile = res[:, i * 1024:(i + 1) * 1024]
                utk[pr] = ve(P, "scalar_tensor_tensor", rtile, rtile, 0.5,
                             sv[:, pr * 1024:(pr + 1) * 1024],
                             ALU.mult, ALU.add,
                             waits=[(f"l2fsv{pr}", svt[pr])])
        P.barrier()


def _l2f_attn(P, env, li, s, cqt, msk, qsb, onec, xn, sq2, stat):
    nc, aps = env.nc, env.aps
    res = env.sb["res"]
    ident = env.sb["ident"]
    t0 = s * NT
    with nc.sbuf_tensor(nm("l2f_qkv"), [128, 3 * 16 * 512], BF16) as qkv:
        with (nc.sbuf_tensor(nm("l2f_xT"), [128, 8 * 2052], BF16) as xT,
              nc.sbuf_tensor(nm("l2f_wq2"), [128, 8 * 1536], BF16) as wq2,
              nc.psum_tensor(nm("l2f_pp"), [128, 2048], F32) as pp,
              nc.psum_tensor(nm("l2f_ptp"), [128, 1024], BF16) as ptp):
            emit_load_w(P, env, None, wq2, 8, 1536,
                        src_ap=aps["l2f_wqkv"][li])
            ve(P, "memset",
               xT[:].rearrange("p (c n) -> p c n", n=2052)[:, :, 0:4], 0.0)
            for i in range(NT):
                itile = res[:, (t0 + i) * 1024:(t0 + i + 1) * 1024]
                emit_transpose_tile(P, itile,
                                    cview(xT, 8, 2052, 4 + i * 128, 128),
                                    ptp, ident, P.cur("ve"))
            P.barrier(engines=("tensor",), sems=("ve", "sdg"))
            qn_tok = 0
            for i in range(NT):
                for g in range(3):
                    for k in range(8):
                        off = 4 + i * 128 - (1 if k >= 4 else 0)
                        mm(P, pp[:, g * 512:(g + 1) * 512],
                           xT[:, k * 2052 + off: k * 2052 + off + 128],
                           wq2[:, k * 1536 + g * 512:
                               k * 1536 + (g + 1) * 512],
                           start=(k == 0), stop=(k == 7),
                           waits=[("ve", qn_tok)]
                           if (g == 0 and k == 0) else ())
                ve(P, "tensor_copy",
                   qkv[:, (32 + i) * 512:(32 + i + 1) * 512],
                   pp[:, 1024:1536], waits=[("pe", P.cur("pe"))])
                for wh in range(2):
                    ps_q = pp[:, wh * 512:(wh + 1) * 512]
                    sc(P, "activation", sq2[:, 0:512], ps_q, AF.Square,
                       waits=[("ve", P.cur("ve"))])
                    ve(P, "reduce_sum", stat[:, 8:16].unsqueeze(2),
                       sq2[:, 0:512].rearrange("p (h d) -> p h d", d=64),
                       axis=AX.X, waits=[("sc", P.cur("sc"))])
                    sc(P, "activation", stat[:, 8:16], stat[:, 8:16],
                       AF.Sqrt, waits=[("ve", P.cur("ve"))])
                    ve(P, "tensor_scalar_max", stat[:, 8:16], stat[:, 8:16],
                       1e-12, waits=[("sc", P.cur("sc"))])
                    ve(P, "reciprocal", stat[:, 8:16], stat[:, 8:16])
                    for h in range(8):
                        ve(P, "tensor_scalar",
                           sq2[:, 512 + h * 64:512 + h * 64 + 64],
                           ps_q[:, h * 64:h * 64 + 64],
                           stat[:, 8 + h:9 + h], None, ALU.mult,
                           waits=[("ve", P.cur("ve"))] if h == 0 else ())
                    qh = sq2[:, 512:1024].rearrange(
                        "p (h two d) -> p h two d", two=2, d=32)
                    dst = qkv[:, (wh * 16 + i) * 512:
                              (wh * 16 + i + 1) * 512]
                    dh = dst.rearrange("p (h two d) -> p h two d",
                                       two=2, d=32)
                    cA = (cqt[:, wh * 1024 + i * 32:wh * 1024 + i * 32 + 32]
                          .unsqueeze(1).broadcast_to([128, 8, 32]))
                    sA = (cqt[:, wh * 1024 + 512 + i * 32:
                              wh * 1024 + 512 + i * 32 + 32]
                          .unsqueeze(1).broadcast_to([128, 8, 32]))
                    qlo = (qsb[:, wh * 64:wh * 64 + 32]
                           .unsqueeze(1).broadcast_to([128, 8, 32]))
                    qhi = (qsb[:, wh * 64 + 32:wh * 64 + 64]
                           .unsqueeze(1).broadcast_to([128, 8, 32]))
                    t1 = sq2[:, 0:256].rearrange("p (h d) -> p h d", d=32)
                    t2 = sq2[:, 256:512].rearrange("p (h d) -> p h d", d=32)
                    ve(P, "tensor_mul", t1, qh[:, :, 0], qlo,
                       waits=[("ve", P.cur("ve"))])
                    ve(P, "tensor_mul", t2, qh[:, :, 1], qhi)
                    ve(P, "tensor_mul", dh[:, :, 0], t1, cA)
                    ve(P, "tensor_mul", qh[:, :, 0], t2, sA)
                    ve(P, "tensor_sub", dh[:, :, 0], dh[:, :, 0],
                       qh[:, :, 0])
                    ve(P, "tensor_mul", dh[:, :, 1], t2, cA)
                    ve(P, "tensor_mul", qh[:, :, 0], t1, sA)
                    qn_tok = ve(P, "tensor_add", dh[:, :, 1], dh[:, :, 1],
                                qh[:, :, 0])
            P.barrier()
        with (nc.sbuf_tensor(nm("l2f_qkT"), [64, 2 * 2048], BF16) as qkT,
              nc.sbuf_tensor(nm("l2f_PT"), [128, 512], BF16) as PT,
              nc.sbuf_tensor(nm("l2f_ao"), [128, 16 * 512], BF16) as ao,
              nc.psum_tensor(nm("l2f_psc"), [128, 512], F32) as psc,
              nc.psum_tensor(nm("l2f_pav"), [128, 512], F32) as pav,
              nc.psum_tensor(nm("l2f_pt2"), [64, 1024], BF16) as pt2):
            for h in range(8):
                for wv2 in range(2):
                    for wave in range(2):
                        for k in range(8):
                            til = wave * 8 + k
                            tr(P, pt2[:, k * 128:(k + 1) * 128],
                               qkv[:, (wv2 * 16 + til) * 512 + h * 64:
                                   (wv2 * 16 + til) * 512 + h * 64 + 64],
                               ident,
                               waits=[("ve", P.cur("ve"))]
                               if k == 0 else ())
                        ve(P, "tensor_copy",
                           qkT[:, wv2 * 2048 + wave * 1024:
                               wv2 * 2048 + (wave + 1) * 1024],
                           pt2[:], waits=[("pe", P.cur("pe"))])
                P.barrier(engines=("tensor",), sems=("ve",))
                ex_tok = 0
                for w in range(4):
                    kcs = list(range(max(0, (w - 1) * 4), (w + 1) * 4))
                    navt = P.cur("ve")
                    for ki, kc in enumerate(kcs):
                        r = kc - 4 * w + 4
                        mm(P, psc[:],
                           qkT[:, 2048 + kc * 128:2048 + kc * 128 + 128],
                           qkT[:, w * 512:(w + 1) * 512],
                           start=True, stop=True,
                           waits=[("sc", ex_tok), ("ve", navt)])
                        mt = ve(P, "tensor_tensor", psc[:], psc[:],
                                msk[:, r * 512:(r + 1) * 512], ALU.min,
                                waits=[("pe", P.cur("pe"))])
                        ex_tok = sc(P, "activation", PT[:], psc[:], AF.Exp,
                                    waits=[("ve", mt),
                                           ("pe", P.cur("pe"))])
                        for qsub in range(4):
                            qt = w * 4 + qsub
                            c0 = qsub * 128
                            mm(P, pav[:, c0 + 64:c0 + 65],
                               PT[:, qsub * 128:qsub * 128 + 128], onec[:],
                               start=(ki == 0 and qsub == 0),
                               stop=False,
                               waits=[("sc", ex_tok)] if qsub == 0 else ())
                            mm(P, pav[:, c0:c0 + 64],
                               PT[:, qsub * 128:qsub * 128 + 128],
                               qkv[:, (32 + kc) * 512 + h * 64:
                                   (32 + kc) * 512 + h * 64 + 64],
                               start=False,
                               stop=(ki == len(kcs) - 1 and qsub == 3))
                    avt = P.cur("pe")
                    for qsub in range(4):
                        qt = w * 4 + qsub
                        c0 = qsub * 128
                        ve(P, "reciprocal", stat[:, 0:1],
                           pav[:, c0 + 64:c0 + 65], waits=[("pe", avt)])
                        ve(P, "tensor_scalar",
                           ao[:, qt * 512 + h * 64:qt * 512 + h * 64 + 64],
                           pav[:, c0:c0 + 64], stat[:, 0:1], None,
                           ALU.mult, waits=[("ve", P.cur("ve"))])
            P.barrier()
            with (nc.sbuf_tensor(nm("l2f_wo2"), [128, 4 * 1024],
                                 BF16) as wo2,
                  nc.psum_tensor(nm("l2f_ptb"), [128, 512], BF16) as ptb,
                  nc.psum_tensor(nm("l2f_py"), [128, 1024], F32) as py):
                emit_load_w(P, env, None, wo2, 4, 1024,
                            src_ap=aps["l2f_wo"][li])
                y_tok = 0
                for i in range(NT):
                    att = ao[:, i * 512:(i + 1) * 512]
                    tt = emit_transpose_tile(P, att, xn[:, 0:512], ptb,
                                             ident, P.cur("ve"), nch=4)
                    for g in range(2):
                        pv = emit_proj(P, py[:, g * 512:(g + 1) * 512],
                                       xn[:, 0:512], wo2, 1024, g * 512,
                                       512, kch=4,
                                       wait_first=[("ve", tt),
                                                   ("ve", y_tok),
                                                   ("sdg", 1)])
                    rtile = res[:, (t0 + i) * 1024:(t0 + i + 1) * 1024]
                    ve(P, "tensor_add", rtile[:, 0:512], rtile[:, 0:512],
                       py[:, 0:512], waits=[("pe", pv)])
                    y_tok = ve(P, "tensor_add", rtile[:, 512:1024],
                               rtile[:, 512:1024], py[:, 512:1024])
            P.barrier()


def _l2f_mlp(P, env, li, s, rowb, lngb, xn, sq2, stat):
    nc, aps = env.nc, env.aps
    res = env.sb["res"]
    ident = env.sb["ident"]
    ones1 = env.sb["ones1"]
    t0 = s * NT
    with (nc.sbuf_tensor(nm("l2f_hsh"), [128, 16 * 512], BF16) as hsh,
          nc.sbuf_tensor(nm("l2f_xnTg"), [128, 8 * 512], BF16) as xnTg,
          nc.sbuf_tensor(nm("l2f_hidT"), [128, 24 * 512], BF16) as hidT,
          nc.sbuf_tensor(nm("l2f_gst"), [128, 512], BF16) as gst,
          nc.sbuf_tensor(nm("l2f_hstg"), [128, 512], BF16) as hstg):
        for i in range(NT):
            dma(P, "sync", hsh[1:128, i * 512:(i + 1) * 512],
                res[0:127, (t0 + i) * 1024 + 512:(t0 + i) * 1024 + 1024])
            if i == 0:
                ve(P, "memset", hsh[0:1, 0:512], 0.0)
            else:
                dma(P, "sync", hsh[0:1, i * 512:i * 512 + 512],
                    res[127:128, (t0 + i - 1) * 1024 + 512:
                        (t0 + i) * 1024])
        P.barrier()
        for g4 in range(4):
            with nc.psum_tensor(nm("l2f_ptp2"), [128, 1024], BF16) as ptp:
                for ti in range(4):
                    i = g4 * 4 + ti
                    rtile = res[:, (t0 + i) * 1024:(t0 + i + 1) * 1024]
                    htile = hsh[:, i * 512:(i + 1) * 512]
                    ve(P, "tensor_tensor", sq2[:, 0:512], rtile[:, 0:512],
                       rtile[:, 0:512], ALU.mult)
                    ve(P, "tensor_tensor", sq2[:, 512:1024], htile, htile,
                       ALU.mult)
                    ve(P, "reduce_sum", stat[:, 2:3], sq2[:], axis=AX.X)
                    ve(P, "reduce_sum", stat[:, 3:4], rtile[:, 0:512],
                       axis=AX.X)
                    ve(P, "reduce_sum", stat[:, 4:5], htile, axis=AX.X)
                    ve(P, "tensor_add", stat[:, 3:4], stat[:, 3:4],
                       stat[:, 4:5])
                    ve(P, "tensor_scalar_mul", stat[:, 3:4], stat[:, 3:4],
                       1.0 / D)
                    ve(P, "tensor_tensor", stat[:, 4:5], stat[:, 3:4],
                       stat[:, 3:4], ALU.mult)
                    ve(P, "tensor_scalar", stat[:, 2:3], stat[:, 2:3],
                       1.0 / D, None, ALU.mult)
                    ve(P, "tensor_sub", stat[:, 2:3], stat[:, 2:3],
                       stat[:, 4:5])
                    ve(P, "tensor_scalar", stat[:, 2:3], stat[:, 2:3], 1.0,
                       EPS, ALU.mult, ALU.add)
                    sc(P, "activation", stat[:, 2:3], stat[:, 2:3],
                       AF.Sqrt, waits=[("ve", P.cur("ve"))])
                    ve(P, "reciprocal", stat[:, 0:1], stat[:, 2:3],
                       waits=[("sc", P.cur("sc"))])
                    ve(P, "tensor_tensor", stat[:, 1:2], stat[:, 3:4],
                       stat[:, 0:1], ALU.mult)
                    ve(P, "tensor_scalar_mul", stat[:, 1:2], stat[:, 1:2],
                       -1.0)
                    ve(P, "tensor_scalar", xn[:, 0:512], rtile[:, 0:512],
                       stat[:, 0:1], stat[:, 1:2], ALU.mult, ALU.add,
                       waits=[("ve", P.cur("ve"))])
                    lt = ve(P, "tensor_scalar", xn[:, 512:1024], htile,
                            stat[:, 0:1], stat[:, 1:2], ALU.mult, ALU.add)
                    emit_transpose_tile(P, xn[:],
                                        cview(xnTg, 8, 512, ti * 128, 128),
                                        ptp, ident, lt,
                                        gcol=lngb[:, 0:8],
                                        bcol=lngb[:, 8:16])
            P.barrier()
            with (nc.sbuf_tensor(nm("l2f_w1b"), [128, 2 * 8 * 1024],
                                 BF16) as w1b,
                  nc.psum_tensor(nm("l2f_pag"), [128, 1024], F32) as pag,
                  nc.psum_tensor(nm("l2f_pt3"), [128, 512], BF16) as pt3):
                P.sem("l2fb0"); P.sem("l2fb1")
                wt = [0, 0]
                w1v = aps["l2f_w1p"][li].rearrange("(c p) n -> p c n",
                                                   p=128)
                for cg in range(6):
                    pr = cg % 2
                    half = w1b[:, pr * 8192:(pr + 1) * 8192].rearrange(
                        "p (k c n) -> p k c n", c=2, n=512)
                    dma(P, "gpsimd", half[:, :, 0, :],
                        w1v[:, :, cg * 512:(cg + 1) * 512],
                        waits=[("pe", P.cur("pe"))], sem=f"l2fb{pr}")
                    wt[pr] = dma(P, "gpsimd", half[:, :, 1, :],
                                 w1v[:, :, 3072 + cg * 512:
                                     3072 + (cg + 1) * 512],
                                 sem=f"l2fb{pr}")
                    for ti in range(4):
                        for ag in range(2):
                            for k in range(8):
                                mm(P, pag[:, ag * 512:(ag + 1) * 512],
                                   xnTg[:, k * 512 + ti * 128:
                                        k * 512 + ti * 128 + 128],
                                   w1b[:, pr * 8192 + k * 1024 + ag * 512:
                                       pr * 8192 + k * 1024 + ag * 512
                                       + 512],
                                   start=(k == 0), stop=False,
                                   waits=[(f"l2fb{pr}", wt[pr]),
                                          ("ve", P.cur("ve")),
                                          ("sc", P.cur("sc"))]
                                   if (ag == 0 and k == 0) else ())
                            mm(P, pag[:, ag * 512:(ag + 1) * 512], ones1,
                               rowb[:, ag * 3072 + cg * 512:
                                    ag * 3072 + (cg + 1) * 512],
                               start=False, stop=True)
                        gt = sc(P, "activation", gst[:], pag[:, 512:1024],
                                AF.Gelu, waits=[("pe", P.cur("pe"))])
                        ht = ve(P, "tensor_tensor", hstg[:],
                                pag[:, 0:512], gst[:], ALU.mult,
                                waits=[("sc", gt)])
                        emit_transpose_tile(
                            P, hstg[:],
                            cview(hidT, 24, 512, ti * 128, 128)
                            [:, cg * 4:(cg + 1) * 4, :],
                            pt3, ident, ht, nch=4)
                P.barrier()
            with (nc.sbuf_tensor(nm("l2f_w2b"), [128, 2 * 1024],
                                 BF16) as w2b,
                  nc.psum_tensor(nm("l2f_py2"), [128, 4096], F32) as py):
                w2v = aps["l2f_w2p"][li].rearrange("(c p) n -> p c n",
                                                   p=128)
                for ch in range(24):
                    pr = ch % 2
                    wt[pr] = dma(P, "gpsimd",
                                 w2b[:, pr * 1024:(pr + 1) * 1024],
                                 w2v[:, ch],
                                 waits=[("pe", P.cur("pe")),
                                        ("ve", P.cur("ve"))]
                                 if ch < 2 else [("pe", P.cur("pe"))],
                                 sem=f"l2fb{pr}")
                    for ti in range(4):
                        for hh in range(2):
                            mm(P, py[:, ti * 1024 + hh * 512:
                                  ti * 1024 + (hh + 1) * 512],
                               hidT[:, ch * 512 + ti * 128:
                                    ch * 512 + ti * 128 + 128],
                               w2b[:, pr * 1024 + hh * 512:
                                   pr * 1024 + (hh + 1) * 512],
                               start=(ch == 0), stop=False,
                               waits=[(f"l2fb{pr}", wt[pr])]
                               if (ti == 0 and hh == 0) else ())
                for ti in range(4):
                    for hh in range(2):
                        mm(P, py[:, ti * 1024 + hh * 512:
                              ti * 1024 + (hh + 1) * 512],
                           ones1, rowb[:, 6144 + hh * 512:
                                       6144 + (hh + 1) * 512],
                           start=False, stop=True)
                ytok = P.cur("pe")
                for ti in range(4):
                    i = g4 * 4 + ti
                    rtile = res[:, (t0 + i) * 1024:(t0 + i + 1) * 1024]
                    ve(P, "tensor_add", rtile[:, 0:512],
                       rtile[:, 0:512], py[:, ti * 1024:ti * 1024 + 512],
                       waits=[("pe", ytok)])
                    ve(P, "tensor_add", rtile[:, 512:1024],
                       rtile[:, 512:1024],
                       py[:, ti * 1024 + 512:ti * 1024 + 1024])
                P.barrier()




def prep_l3_ffn(inp, c):
    if "l3f_seL" in c:
        return
    c["l3f_seL"] = _f32(np.asarray(inp["L3_ffn_slot_embeds"])
                        .reshape(1024, 1024))
    c["l3f_Wg"] = _f32(inp["L3_ffn_Wg"])
    c["l3f_Wu"] = _f32(inp["L3_ffn_Wu"])
    c["l3f_Wd"] = _f32(inp["L3_ffn_Wd"])
    c["l3f_bg"] = _f32(np.asarray(inp["L3_ffn_bg"]).reshape(16, 1, 2048))
    c["l3f_bu"] = _f32(np.asarray(inp["L3_ffn_bu"]).reshape(16, 1, 2048))
    c["l3f_bd"] = _f32(np.asarray(inp["L3_ffn_bd"]).reshape(16, 1, 1024))
    c["l3f_ng"] = fm_layout(inp["L3_ffn_norm_g"])
    c["l3f_nb"] = fm_layout(inp["L3_ffn_norm_b"])
    c["l3f_sg"] = fm_layout(inp["L3_ffn_slot_g"])
    c["l3f_sb"] = fm_layout(inp["L3_ffn_slot_b"])
    c["l3f_scale_c"] = _f32(np.broadcast_to(
        np.asarray(inp["L3_ffn_scale"]).reshape(1), (128, 1)))


L3F_STAGE = [9]
L3F_SUB = [9]


def emit_l3_ffn(P, env):
    nc, aps = env.nc, env.aps
    res = env.sb["res"]
    ident = env.sb["ident"]
    identf = env.sb["identf"]
    ones1 = env.sb["ones1"]
    onescol = env.sb["onescol"]
    onesrow = env.sb["onesrow"]
    c_d = env.scratch("l3f_c_d", [SEQS, 16, 64, 2048], BF16)

    with ExitStack() as ctx:
        e = ctx.enter_context
        af = e(nc.sbuf_tensor(nm("l3f_af"), [128, 48], F32))
        sq3 = e(nc.sbuf_tensor(nm("l3f_sq"), [128, 1024], F32))
        stat = e(nc.sbuf_tensor(nm("l3f_st"), [128, 16], F32))
        cmv = e(nc.sbuf_tensor(nm("l3f_cmv"), [128, 16], F32))
        crecT = e(nc.sbuf_tensor(nm("l3f_cr"), [128, 2 * 16], F32))
        sTall = e(nc.sbuf_tensor(nm("l3f_sTa"), [128, 8 * 2048], BF16))
        ctx2 = ctx.enter_context(ExitStack())
        seT = ctx2.enter_context(
            nc.sbuf_tensor(nm("l3f_seT"), [128, 8 * 1024], BF16))
        rrow = ctx2.enter_context(
            nc.sbuf_tensor(nm("l3f_rr"), [1, 1024], BF16))
        xn = ctx2.enter_context(
            nc.sbuf_tensor(nm("l3f_xn"), [128, 1024], BF16))
        Lb = ctx2.enter_context(
            nc.sbuf_tensor(nm("l3f_lb"), [128, 2048], BF16))
        mtt = ctx2.enter_context(
            nc.sbuf_tensor(nm("l3f_mtt"), [128, 17], F32))
        mrow = ctx2.enter_context(
            nc.sbuf_tensor(nm("l3f_mr"), [1, 2048], BF16))
        pnms = ctx2.enter_context(
            nc.sbuf_tensor(nm("l3f_pnm"), [16, 128], F32))
        P.barrier()
        dma(P, "gpsimd", af[:, 0:8], aps["l3f_ng"])
        dma(P, "gpsimd", af[:, 8:16], aps["l3f_nb"])
        dma(P, "gpsimd", af[:, 32:40], aps["l3f_sg"])
        dma(P, "gpsimd", af[:, 40:48], aps["l3f_sb"])
        sct = dma(P, "gpsimd", stat[:, 15:16], aps["l3f_scale_c"])
        ve(P, "tensor_scalar", af[:, 16:24], af[:, 32:40], stat[:, 15:16],
           None, ALU.mult, waits=[("sdg", sct), ("ve", P.cur("ve"))])
        ve(P, "tensor_scalar", af[:, 24:32], af[:, 40:48], stat[:, 15:16],
           None, ALU.mult, waits=[("ve", P.cur("ve"))])

        with (nc.sbuf_tensor(nm("l3f_sef"), [128, 1024], F32) as sef,
              nc.psum_tensor(nm("l3f_pse"), [128, 1024], F32) as pse,
              nc.psum_tensor(nm("l3f_ptp0"), [128, 1024], BF16) as ptp0):
            for u in range(8):
                set_tok = dma(P, "gpsimd", sef[:],
                              aps["l3f_seL"][u * 128:(u + 1) * 128, :],
                              waits=[("ve", P.cur("ve")),
                                     ("pe", P.cur("pe"))])
                emit_ln_stats(P, sef[:], sq3[:, 0:1024], stat, 0,
                              waits=[("sdg", set_tok)])
                lt = emit_ln_apply(P, sef[:], stat, 0, xn[:])
                emit_transpose_tile(
                    P, xn[:], cview(seT, 8, 1024, u * 128, 128), ptp0,
                    ident, lt, gcol=af[:, 16:24], bcol=af[:, 24:32])
            ve(P, "tensor_copy", xn[:, 0:8], af[:, 8:16],
               waits=[("ve", P.cur("ve"))])
            for dch in range(8):
                for rh in range(2):
                    mm(P, pse[0:1, rh * 512:(rh + 1) * 512],
                       xn[:, dch:dch + 1],
                       seT[:, dch * 1024 + rh * 512:
                           dch * 1024 + (rh + 1) * 512],
                       start=(dch == 0), stop=(dch == 7),
                       waits=[("ve", P.cur("ve"))]
                       if (dch == 0 and rh == 0) else ())
            ve(P, "tensor_copy", rrow[:], pse[0:1, 0:1024],
               waits=[("pe", P.cur("pe"))])
            for dch in range(8):
                ve(P, "tensor_scalar", seT[:, dch * 1024:(dch + 1) * 1024],
                   seT[:, dch * 1024:(dch + 1) * 1024],
                   af[:, dch:dch + 1], None, ALU.mult,
                   waits=[("ve", P.cur("ve"))] if dch == 0 else ())
        P.barrier()

        for s in range(SEQS):
            t0 = s * NT
            with (nc.sbuf_tensor(nm("l3f_sst"), [128, 32], F32) as sst,
                  nc.sbuf_tensor(nm("l3f_xnT"), [128, 8 * 2048],
                                 BF16) as xnT,
                  nc.sbuf_tensor(nm("l3f_dtm"), [128, 16 * 1024],
                                 BF16) as dtm,
                  nc.sbuf_tensor(nm("l3f_dt"), [128, 2048], BF16) as DT,
                  nc.psum_tensor(nm("l3f_pl"), [128, 2048], F32) as pl):

                def logits_mms(u, close=True):
                    for tg in range(4):
                        for dch in range(8):
                            mm(P, pl[:, tg * 512:(tg + 1) * 512],
                               seT[:, dch * 1024 + u * 128:
                                   dch * 1024 + u * 128 + 128],
                               xnT[:, dch * 2048 + tg * 512:
                                   dch * 2048 + (tg + 1) * 512],
                               start=(dch == 0), stop=False,
                               waits=[("sc", P.cur("sc")),
                                      ("ve", P.cur("ve"))]
                               if (tg == 0 and dch == 0) else ())
                        mm(P, pl[:, tg * 512:(tg + 1) * 512],
                           rrow[0:1, u * 128:u * 128 + 128],
                           onesrow[0:1, 0:512],
                           start=False, stop=close)

                with nc.psum_tensor(nm("l3f_ptp"), [128, 1024],
                                    BF16) as ptp:
                    for i in range(NT):
                        rtile = res[:, (t0 + i) * 1024:(t0 + i + 1) * 1024]
                        emit_ln_stats(P, rtile, sq3[:, 0:1024], stat, 0,
                                      double_ln=True)
                        ve(P, "tensor_copy", sst[:, 2 * i:2 * i + 2],
                           stat[:, 0:2], waits=[("ve", P.cur("ve"))])
                        lt = emit_ln_apply(P, rtile, stat, 0, xn[:])
                        emit_transpose_tile(
                            P, xn[:], cview(xnT, 8, 2048, i * 128, 128),
                            ptp, ident, lt)
                    P.barrier()
                    ve(P, "memset", mtt[:], 130.0 if L3F_SUB[0] < 6
                       else -1e30)
                    for u in range(8):
                        logits_mms(u)
                        ve(P, "reduce_max", cmv[:, u:u + 1], pl[:],
                           axis=AX.X, waits=[("pe", P.cur("pe"))])
                        if L3F_SUB[0] >= 6:
                            ve(P, "tensor_copy", Lb[:], pl[:])
                        ve(P, "tensor_scalar_mul", cmv[:, 8 + u:9 + u],
                           cmv[:, u:u + 1], -1.0)
                        et = sc(P, "activation", DT[:], pl[:], AF.Exp,
                                cmv[:, 8 + u:9 + u], 1.0,
                                waits=[("ve", P.cur("ve")),
                                       ("pe", P.cur("pe"))])
                        ve(P, "reduce_sum", stat[:, 0:1], DT[:], axis=AX.X,
                           waits=[("sc", et)])
                        ve(P, "reciprocal", stat[:, 1:2], stat[:, 0:1])
                        dn = ve(P, "tensor_scalar", DT[:], DT[:],
                                stat[:, 1:2], None, ALU.mult,
                                waits=[("ve", P.cur("ve"))])
                        for half in range(2):
                            for k in range(8):
                                tr(P, ptp[:, k * 128:(k + 1) * 128],
                                   DT[:, half * 1024 + k * 128:
                                      half * 1024 + k * 128 + 128], ident,
                                   waits=[("ve", dn if half == 0 else
                                           P.cur("ve"))]
                                   if k == 0 else ())
                            ve(P, "tensor_copy",
                               cview(dtm, 16, 1024, u * 128, 128)
                               [:, half * 8:(half + 1) * 8, :],
                               ptp[:].rearrange("p (c n) -> p c n", n=128),
                               waits=[("pe", P.cur("pe"))])
                        for half in range(2 if L3F_SUB[0] >= 6 else 0):
                            for k in range(8):
                                tr(P, ptp[:, k * 128:(k + 1) * 128],
                                   Lb[:, half * 1024 + k * 128:
                                      half * 1024 + k * 128 + 128], ident,
                                   waits=[("ve", P.cur("ve"))]
                                   if k == 0 else ())
                            ptk2 = P.cur("pe")
                            for k in range(8):
                                til = half * 8 + k
                                ve(P, "reduce_max", mtt[:, 16:17],
                                   ptp[:, k * 128:(k + 1) * 128],
                                   axis=AX.X,
                                   waits=[("pe", ptk2)] if k == 0 else ())
                                ve(P, "tensor_max", mtt[:, til:til + 1],
                                   mtt[:, til:til + 1], mtt[:, 16:17],
                                   waits=[("ve", P.cur("ve"))])
                    if L3F_SUB[0] < 5:
                        ve(P, "memset", mrow[0:1, 0:2048], -130.0)
                    P.sem("l3fmr")
                    mt = P.cur("l3fmr")
                    if L3F_SUB[0] >= 5:
                        ve(P, "tensor_scalar_mul", mtt[:, 0:16],
                           mtt[:, 0:16], -1.0,
                           waits=[("pe", P.cur("pe"))])
                    if L3F_SUB[0] >= 5:
                        tr(P, pl[0:16, 0:128], mtt[:, 0:16], identf,
                           waits=[("ve", P.cur("ve"))])
                        ve(P, "tensor_copy", pnms[:], pl[0:16, 0:128],
                           waits=[("pe", P.cur("pe")),
                                  ("l3fmr", P.cur("l3fmr"))])
                        mt = dma(P, "gpsimd",
                                 mrow[0:1, 0:2048]
                                 .rearrange("o (c n) -> o c n", n=128),
                                 pnms[:], waits=[("ve", P.cur("ve"))],
                                 sem="l3fmr")
                with nc.psum_tensor(nm("l3f_pcs"), [128, 32], F32) as pcs:
                    for u in range(8 if L3F_STAGE[0] >= 2 else 0):
                        logits_mms(u, close=(L3F_SUB[0] < 1))
                        for tg in range(4 if L3F_SUB[0] >= 1 else 0):
                            mm(P, pl[:, tg * 512:(tg + 1) * 512], ones1,
                               mrow[0:1, tg * 512:(tg + 1) * 512],
                               start=False, stop=True,
                               waits=[("l3fmr", mt)]
                               if (u == 0 and tg == 0) else ())
                        et = sc(P, "activation", DT[:], pl[:], AF.Exp,
                                waits=[("pe", P.cur("pe")),
                                       ("ve", P.cur("ve")), ("sds", 1)])
                        for til in range(NT if L3F_SUB[0] >= 2 else 0):
                            mm(P, pcs[:, 2 * til:2 * til + 1],
                               DT[:, til * 128:(til + 1) * 128], onescol,
                               start=(u == 0 and til == 0),
                               stop=(u == 7 and til == 15),
                               waits=[("sc", et)] if til == 0 else ())
                        if L3F_SUB[0] >= 3:
                            dma(P, "sync", c_d.ap()[s, 2 * u], DT[0:64, :],
                                waits=[("sc", et)])
                            dma(P, "sync", c_d.ap()[s, 2 * u + 1],
                                DT[64:128, :], waits=[("sc", et)])
                    if L3F_STAGE[0] >= 2 and L3F_SUB[0] >= 2:
                        ve(P, "reciprocal", crecT[:, s * 16:s * 16 + 16],
                           pcs[:, 0:32:2], waits=[("pe", P.cur("pe"))])
                        ve(P, "tensor_scalar_mul",
                           crecT[:, s * 16:s * 16 + 16],
                           crecT[:, s * 16:s * 16 + 16], 0.5,
                           waits=[("ve", P.cur("ve"))])
                    else:
                        ve(P, "memset", crecT[:, s * 16:s * 16 + 16], 1.0,
                           waits=[("pe", P.cur("pe"))])
                P.barrier()
                with nc.psum_tensor(nm("l3f_psl"), [128, 2048], F32) as psl:
                    for pas in range(4 if L3F_STAGE[0] >= 3 else 0):
                        for i in range(NT):
                            rtile = res[:, (t0 + i) * 1024:
                                        (t0 + i + 1) * 1024]
                            xt = ve(P, "tensor_scalar", xn[:], rtile,
                                    sst[:, 2 * i:2 * i + 1],
                                    sst[:, 2 * i + 1:2 * i + 2],
                                    ALU.mult, ALU.add,
                                    waits=[("ve", P.cur("ve")),
                                           ("pe", P.cur("pe"))])
                            for r2 in range(2):
                                dsl = pas * 2 + r2
                                for slg in range(2):
                                    mm(P, psl[:, r2 * 1024 + slg * 512:
                                          r2 * 1024 + (slg + 1) * 512],
                                       xn[:, dsl * 128:dsl * 128 + 128],
                                       dtm[:, i * 1024 + slg * 512:
                                           i * 1024 + (slg + 1) * 512],
                                       start=(i == 0), stop=(i == NT - 1),
                                       waits=[("ve", xt)]
                                       if (r2 == 0 and slg == 0)
                                       else ())
                        ptk = P.cur("pe")
                        for r2 in range(2):
                            dsl = pas * 2 + r2
                            dstv = (sTall[:, dsl * 2048:(dsl + 1) * 2048]
                                    .rearrange("p (e n) -> p e n", n=128)
                                    [:, :, s * 64:(s + 1) * 64])
                            srcv = (psl[:, r2 * 1024:(r2 + 1) * 1024]
                                    .rearrange("p (e n) -> p e n", n=64))
                            ve(P, "tensor_scalar", dstv, srcv,
                               af[:, dsl:dsl + 1], af[:, 8 + dsl:9 + dsl],
                               ALU.mult, ALU.add,
                               waits=[("pe", ptk), ("ve", P.cur("ve"))])
                    P.barrier()
            P.barrier()

        ctx2.close()
        with (nc.sbuf_tensor(nm("l3f_eo0"), [64, 16 * 1024], BF16) as eo0,
              nc.sbuf_tensor(nm("l3f_eo1"), [64, 16 * 1024], BF16) as eo1):
          with (nc.sbuf_tensor(nm("l3f_wb"), [128, 16384], BF16) as wb3,
                nc.sbuf_tensor(nm("l3f_gu"), [128, 2 * 2048], BF16) as gu,
                nc.psum_tensor(nm("l3f_pg"), [128, 2048], F32) as pg,
                nc.psum_tensor(nm("l3f_pd"), [128, 1024], F32) as pd,
                nc.psum_tensor(nm("l3f_pth"), [128, 1024], BF16) as pth):
              hT = gu[:, 2048:4096]
              est = gu[:, 0:1024]
              brw = sq3[0:1, 0:1024].bitcast(BF16)
              P.sem("l3fw")
              for ex in range(16 if L3F_STAGE[0] >= 4 else 0):
                  for gi, (wn, bn) in enumerate((("l3f_Wg", "l3f_bg"),
                                                 ("l3f_Wu", "l3f_bu"))):
                      wt3 = dma(P, "gpsimd",
                                wb3[:, 0:8 * 2048]
                                .rearrange("p (c n) -> p c n", n=2048),
                                aps[wn][ex].rearrange("(c p) n -> p c n",
                                                      p=128),
                                waits=[("pe", P.cur("pe"))], sem="l3fw")
                      bt3 = dma(P, "gpsimd", brw[:], aps[bn][ex], sem="l3fw")
                      for fg in range(4):
                          for dch in range(8):
                              mm(P, pg[:, fg * 512:(fg + 1) * 512],
                                 sTall[:, dch * 2048 + ex * 128:
                                       dch * 2048 + ex * 128 + 128],
                                 wb3[:, dch * 2048 + fg * 512:
                                     dch * 2048 + (fg + 1) * 512],
                                 start=(dch == 0), stop=False,
                                 waits=[("l3fw", bt3), ("sc", P.cur("sc")),
                                        ("ve", P.cur("ve"))]
                                 if (fg == 0 and dch == 0) else ())
                          mm(P, pg[:, fg * 512:(fg + 1) * 512], ones1,
                             brw[0:1, fg * 512:(fg + 1) * 512],
                             start=False, stop=True)
                      ptk = P.cur("pe")
                      if gi == 0:
                          sc(P, "activation", gu[:, 0:2048], pg[:], AF.Gelu,
                             waits=[("pe", ptk), ("ve", P.cur("ve")),
                                    ("sds", 1)])
                      else:
                          sc(P, "activation", gu[:, 2048:4096], pg[:],
                             AF.Copy, waits=[("pe", ptk),
                                             ("ve", P.cur("ve"))])
                  ht = ve(P, "tensor_mul", gu[:, 0:2048], gu[:, 0:2048],
                          gu[:, 2048:4096], waits=[("sc", P.cur("sc"))])
                  for k in range(16):
                      tr(P, pth[:, (k % 8) * 128:(k % 8) * 128 + 128],
                         gu[:, k * 128:(k + 1) * 128], ident,
                         waits=[("ve", P.cur("ve"))] if k % 8 == 0 else (),
                         inc="pe")
                      if k % 8 == 7:
                          ve(P, "tensor_copy",
                             hT[:, (k - 7) * 128:(k + 1) * 128],
                             pth[:].rearrange("p (c n) -> p c n", n=128)
                             .rearrange("p c n -> p (c n)") if False else
                             pth[:], waits=[("pe", P.cur("pe"))])
                  wt3 = dma(P, "gpsimd",
                            wb3[:, 0:16 * 1024]
                            .rearrange("p (c n) -> p c n", n=1024),
                            aps["l3f_Wd"][ex].rearrange("(c p) n -> p c n",
                                                        p=128),
                            waits=[("pe", P.cur("pe"))], sem="l3fw")
                  bt3 = dma(P, "gpsimd", brw[:, 0:1024], aps["l3f_bd"][ex],
                            sem="l3fw")
                  for hh in range(2):
                      for ch in range(16):
                          mm(P, pd[:, hh * 512:(hh + 1) * 512],
                             hT[:, ch * 128:(ch + 1) * 128],
                             wb3[:, ch * 1024 + hh * 512:
                                 ch * 1024 + (hh + 1) * 512],
                             start=(ch == 0), stop=False,
                             waits=[("l3fw", bt3), ("ve", P.cur("ve"))]
                             if ch == 0 else ())
                      mm(P, pd[:, hh * 512:(hh + 1) * 512], ones1,
                         brw[0:1, hh * 512:(hh + 1) * 512],
                         start=False, stop=True)
                  edt = ve(P, "tensor_copy", est[:], pd[:],
                           waits=[("pe", P.cur("pe")), ("sds", 1)])
                  ve(P, "tensor_copy", eo0[:, ex * 1024:(ex + 1) * 1024],
                     est[0:64, :])
                  dma(P, "sync", _sb_slice_ap(eo1, ex), est[64:128, :],
                      waits=[("ve", P.cur("ve"))])
              P.barrier()
          with (nc.sbuf_tensor(nm("l3f_cl"), [64, 2048], BF16) as cl,
                nc.psum_tensor(nm("l3f_po"), [128, 1024], F32) as po):
              for s in range(SEQS):
                  t0 = s * NT
                  eob = eo0 if s == 0 else eo1
                  for i in range(NT):
                      clt = dma(P, "gpsimd",
                                cl[:].rearrange("p (c n) -> p c n",
                                                n=128),
                                c_d.ap()[s, :, :,
                                         i * 128:(i + 1) * 128]
                                .rearrange("c p n -> p c n"),
                                waits=[("pe", P.cur("pe"))])
                      for ex in range(16):
                          for hh in range(2):
                              mm(P, po[:, hh * 512:(hh + 1) * 512],
                                 cl[:, ex * 128:(ex + 1) * 128],
                                 eob[:, ex * 1024 + hh * 512:
                                     ex * 1024 + (hh + 1) * 512],
                                 start=(ex == 0), stop=(ex == 15),
                                 waits=[("sdg", clt),
                                        ("ve", P.cur("ve"))]
                                 if (ex == 0 and hh == 0) else ())
                      rtile = res[:, (t0 + i) * 1024:(t0 + i + 1) * 1024]
                      ve(P, "scalar_tensor_tensor", sq3[:, 0:1024],
                         po[:], crecT[:, s * 16 + i:s * 16 + i + 1],
                         rtile, ALU.mult, ALU.add,
                         waits=[("pe", P.cur("pe"))])
                      ve(P, "tensor_copy", rtile, sq3[:, 0:1024])
        P.barrier()


def _sb_slice_ap(t, ex):
    return t[:, ex * 1024:(ex + 1) * 1024]


def prep_l1_attn(inp, c):
    c["l1a_WqT"] = _f32(np.ascontiguousarray(np.asarray(inp["L1_attn_Wq"]).T))
    c["l1a_WkT"] = _f32(np.ascontiguousarray(np.asarray(inp["L1_attn_Wk"]).T))
    c["l1a_WvT"] = _f32(np.ascontiguousarray(np.asarray(inp["L1_attn_Wv"]).T))
    c["l1a_WgT"] = _f32(np.ascontiguousarray(np.asarray(inp["L1_attn_Wg"]).T))
    c["l1a_WoT"] = _f32(np.ascontiguousarray(np.asarray(inp["L1_attn_Wo"]).T))
    c["l1a_bg"] = _f32(np.asarray(inp["L1_attn_bg"]).reshape(1, 8))
    c["l1a_gain"] = _f32(np.broadcast_to(
        np.asarray(inp["L1_attn_qk_gain"]).reshape(1), (128, 1)))
    if "C_l1a_bm" not in c:
        i = np.arange(128)
        m = np.zeros((128, 128), np.float32)
        blk = (i[:, None] // 64) == (i[None, :] // 64)
        causal = i[:, None] <= i[None, :]
        c["C_l1a_bm"] = _bf(np.where(blk & causal, 900.0, -1e30))
        ind = np.zeros((128, 2), np.float32)
        ind[0:64, 0] = 1.0
        ind[64:128, 1] = 1.0
        c["C_l1a_ind"] = _f32(ind)


def emit_l1_attn(P, env):
    nc, aps = env.nc, env.aps
    res = env.sb["res"]
    ident = env.sb["ident"]
    ones1 = env.sb["ones1"]
    onec = env.sb["onescol"]
    q_d = env.scratch("l1a_q_d", [16, 128, 1024], BF16)
    k_d = env.scratch("l1a_k_d", [16, 128, 1024], BF16)
    ao_d = env.scratch("l1a_ao_d", [16, 128, 1024], BF16)
    for s in ("l1aq0", "l1aq1", "l1ak0", "l1ak1", "l1ahs", "l1ako",
              "l1aao", "l1aai", "l1agr"):
        P.sem(s)

    with ExitStack() as ctx:
        e = ctx.enter_context
        gb = e(nc.sbuf_tensor(nm("l1a_gb"), [128, 256], F32))
        gstg = e(nc.sbuf_tensor(nm("l1a_gst"), [8, 32], F32))
        grow = e(nc.sbuf_tensor(nm("l1a_grw"), [1, 256], BF16))
        bm = e(nc.sbuf_tensor(nm("l1a_bm"), [128, 128], BF16))
        indt = e(nc.sbuf_tensor(nm("l1a_ind"), [128, 2], F32))
        gain = e(nc.sbuf_tensor(nm("l1a_gain"), [128, 1], F32))
        bgr = e(nc.sbuf_tensor(nm("l1a_bgr"), [1, 8], BF16))
        xn = e(nc.sbuf_tensor(nm("l1a_xn"), [128, 1024], BF16))
        xnT = e(nc.sbuf_tensor(nm("l1a_xnT"), [128, 1024], BF16))
        qf = e(nc.sbuf_tensor(nm("l1a_qf"), [128, 1024], F32))
        sq = e(nc.sbuf_tensor(nm("l1a_sq"), [128, 1024], F32))
        stat = e(nc.sbuf_tensor(nm("l1a_st"), [128, 16], F32))
        P.barrier()
        dma(P, "gpsimd", bm[:], aps["C_l1a_bm"])
        dma(P, "gpsimd", indt[:], aps["C_l1a_ind"])
        dma(P, "gpsimd", gain[:], aps["l1a_gain"])
        dma(P, "gpsimd", bgr[:], aps["l1a_bg"])

        for s in range(SEQS):
            t0 = s * NT
            with (nc.sbuf_tensor(nm("l1a_val"), [128, 16 * 1024],
                                 BF16) as val,
                  nc.sbuf_tensor(nm("l1a_aoh"), [128, 2048], BF16) as aoh):
                with (nc.sbuf_tensor(nm("l1a_w4"), [128, 3 * 8192 + 64],
                                     BF16) as w4,
                      nc.psum_tensor(nm("l1a_pp"), [128, 1024], F32) as pp,
                      nc.psum_tensor(nm("l1a_pg"), [128, 32], F32) as pgg,
                      nc.psum_tensor(nm("l1a_ptp"), [128, 1024],
                                     BF16) as ptp):
                    wq = w4[:, 0:8192]
                    wk = w4[:, 8192:16384]
                    wv = w4[:, 16384:24576]
                    wg = w4[:, 24576:24576 + 64]
                    emit_load_w(P, env, "l1a_WqT", wq, 8, 1024)
                    emit_load_w(P, env, "l1a_WkT", wk, 8, 1024)
                    emit_load_w(P, env, "l1a_WvT", wv, 8, 1024)
                    dma(P, "gpsimd", wg.rearrange("p (c n) -> p c n", n=8),
                        aps["l1a_WgT"].rearrange("(c p) n -> p c n", p=128))
                    P.barrier(engines=("tensor", "vector"), sems=("sdg",))
                    st_tok = {"q": [0, 0], "k": [0, 0]}
                    for i in range(NT):
                        par = i % 2
                        rtile = res[:, (t0 + i) * 1024:(t0 + i + 1) * 1024]
                        emit_ln_stats(P, rtile, sq[:], stat, 0,
                                      waits=[("sc", P.cur("sc"))])
                        lt = emit_ln_apply(
                            P, rtile, stat, 0, xn[:],
                            waits=[("l1ak%d" % (1 - par),
                                    st_tok["k"][1 - par])])
                        tt = emit_transpose_tile(P, xn[:], xnT[:], ptp,
                                                 ident, lt)
                        for g in range(2):
                            pv = emit_proj(P, pp[:, g * 512:(g + 1) * 512],
                                           xnT, wv, 1024, g * 512, 512,
                                           wait_first=[("ve", tt)])
                        vev = ve(P, "tensor_copy",
                                 val[:, i * 1024:(i + 1) * 1024], pp[:],
                                 waits=[("pe", pv)])
                        for k in range(8):
                            mm(P, pp[:, 1016:1024],
                               xnT[:, k * 128:(k + 1) * 128],
                               wg[:, k * 8:(k + 1) * 8],
                               start=(k == 0), stop=False,
                               waits=[("ve", vev)] if k == 0 else ())
                        zt = mm(P, pp[:, 1016:1024], ones1, bgr[:],
                                start=False, stop=True)
                        sc(P, "activation", sq[:, 0:8], pp[:, 1016:1024],
                           AF.Abs, waits=[("pe", zt), ("ve", P.cur("ve"))])
                        sc(P, "activation", sq[:, 8:16], sq[:, 0:8],
                           AF.Exp, 0.0, -1.0, waits=[("sc", P.cur("sc"))])
                        ve(P, "tensor_scalar", sq[:, 8:16], sq[:, 8:16],
                           1.0, None, ALU.add,
                           waits=[("sc", P.cur("sc"))])
                        sc(P, "activation", sq[:, 16:24], sq[:, 8:16],
                           AF.Ln, waits=[("ve", P.cur("ve"))])
                        ve(P, "tensor_scalar_min", sq[:, 24:32],
                           pp[:, 1016:1024], 0.0,
                           waits=[("sc", P.cur("sc"))])
                        lg = ve(P, "tensor_sub", sq[:, 0:8], sq[:, 24:32],
                                sq[:, 16:24])
                        mm(P, pgg[0:8, 2 * i:2 * i + 2], sq[:, 0:8],
                           indt[:], start=(i == 0), stop=(i == NT - 1),
                           waits=[("ve", lg)])
                        for wn, wmat, dd in (("q", wq, q_d), ("k", wk, k_d)):
                            sname = "l1a" + wn + str(par)
                            for g in range(2):
                                pv = emit_proj(
                                    P, pp[:, g * 512:(g + 1) * 512],
                                    xnT, wmat, 1024, g * 512, 512,
                                    wait_first=[("ve", P.cur("ve"))])
                            ev = ve(P, "tensor_copy", qf[:], pp[:, 0:1024],
                                    waits=[("pe", pv)])
                            sc(P, "activation", sq[:], qf[:], AF.Square,
                               waits=[("ve", ev)])
                            ve(P, "reduce_sum", stat[:, 0:8].unsqueeze(2),
                               sq[:].rearrange("p (h d) -> p h d", d=128),
                               axis=AX.X, waits=[("sc", P.cur("sc"))])
                            ve(P, "tensor_scalar", stat[:, 0:8],
                               stat[:, 0:8], 1.0 / 128.0, 1e-6, ALU.mult,
                               ALU.add)
                            sc(P, "activation", stat[:, 0:8], stat[:, 0:8],
                               AF.Sqrt, waits=[("ve", P.cur("ve"))])
                            ve(P, "reciprocal", stat[:, 0:8], stat[:, 0:8],
                               waits=[("sc", P.cur("sc"))])
                            if wn == "q":
                                ve(P, "tensor_scalar", stat[:, 0:8],
                                   stat[:, 0:8], gain[:, 0:1], None,
                                   ALU.mult)
                            w0 = [(sname, st_tok[wn][par])]
                            if wn == "k":
                                w0.append(("l1aq%d" % par,
                                           st_tok["q"][par]))
                            for h in range(8):
                                ve(P, "tensor_scalar",
                                   xn[:, h * 128:(h + 1) * 128],
                                   qf[:, h * 128:(h + 1) * 128],
                                   stat[:, h:h + 1], None, ALU.mult,
                                   waits=w0 if h == 0 else ())
                            st_tok[wn][par] = dma(
                                P, "sync", dd.ap()[i], xn[:],
                                waits=[("ve", P.cur("ve"))], sem=sname)
                    et = sc(P, "activation", gstg[:, 0:32], pgg[0:8, 0:32],
                            AF.Exp, waits=[("pe", P.cur("pe")),
                                           ("ve", P.cur("ve"))])
                    gt2 = dma(P, "gpsimd",
                              grow[0:1, 0:256]
                              .rearrange("o (c n) -> o c n", n=32),
                              gstg[:], waits=[("sc", et)], sem="l1agr")
                    mm(P, pp[:, 0:256], ones1, grow[0:1, 0:256],
                       start=True, stop=True,
                       waits=[("l1agr", gt2), ("ve", P.cur("ve"))])
                    ve(P, "tensor_copy", gb[:], pp[:, 0:256],
                       waits=[("pe", P.cur("pe"))])
                    P.barrier()

                with (nc.sbuf_tensor(nm("l1a_hs"), [128, 4096],
                                     BF16) as hs,
                      nc.sbuf_tensor(nm("l1a_qkT"), [128, 4096],
                                     BF16) as qkT,
                      nc.sbuf_tensor(nm("l1a_klo"), [64, 2048],
                                     BF16) as klo,
                      nc.sbuf_tensor(nm("l1a_vlo"), [64, 2048],
                                     BF16) as vlo,
                      nc.sbuf_tensor(nm("l1a_PT"), [128, 128], BF16) as PT,
                      nc.sbuf_tensor(nm("l1a_Sf"), [128, 256], F32) as Sf,
                      nc.sbuf_tensor(nm("l1a_Sb"), [128, 128], BF16) as Sb,
                      nc.psum_tensor(nm("l1a_sc2"), [128, 128], F32) as psc,
                      nc.psum_tensor(nm("l1a_yin"), [128, 136],
                                     F32) as pyin,
                      nc.psum_tensor(nm("l1a_yc"), [128, 128], F32) as pyc,
                      nc.psum_tensor(nm("l1a_kv"), [128, 128], F32) as pkv,
                      nc.psum_tensor(nm("l1a_pt2"), [128, 1024],
                                     BF16) as pt2):
                    ao_tok = 0
                    for h in range(8):
                        hsl = slice(h * 128, (h + 1) * 128)
                        hs_tok = dma(P, "gpsimd",
                                     hs[:, 0:2048]
                                     .rearrange("p (c n) -> p c n", n=128),
                                     q_d.ap()[:, :, hsl]
                                     .rearrange("c p n -> p c n"),
                                     waits=[("pe", P.cur("pe"))],
                                     sem="l1ahs")
                        hs_tok = dma(P, "gpsimd",
                                     hs[:, 2048:4096]
                                     .rearrange("p (c n) -> p c n", n=128),
                                     k_d.ap()[:, :, hsl]
                                     .rearrange("c p n -> p c n"),
                                     sem="l1ahs")
                        ko_tok = dma(P, "sync",
                                     klo[:].rearrange("p (c n) -> p c n",
                                                      n=128),
                                     k_d.ap()[:, 64:128, hsl]
                                     .rearrange("c p n -> p c n"),
                                     waits=[("pe", P.cur("pe"))],
                                     sem="l1ako")
                        ko_tok = dma(P, "sync",
                                     vlo[:].rearrange("p (c n) -> p c n",
                                                      n=128),
                                     val[64:128, :]
                                     .rearrange("p (c n) -> p c n",
                                                n=1024)[:, :, hsl],
                                     waits=[("ve", P.cur("ve"))],
                                     sem="l1ako")
                        for wv2 in range(4):
                            emit_transpose_tile(
                                P, hs[:, wv2 * 1024:(wv2 + 1) * 1024],
                                qkT[:, wv2 * 1024:(wv2 + 1) * 1024],
                                pt2, ident, P.cur("ve"),
                                wait_extra=[("l1ahs", hs_tok)])
                        qT = qkT[:, 0:2048]
                        kT = qkT[:, 2048:4096]
                        ve(P, "memset", Sf[:, 0:128], 0.0,
                           waits=[("pe", P.cur("pe"))])
                        ex_tok = 0
                        for i in range(NT):
                            mm(P, psc[:], kT[:, i * 128:(i + 1) * 128],
                               qT[:, i * 128:(i + 1) * 128],
                               start=True, stop=True,
                               waits=[("sc", ex_tok)])
                            mk = ve(P, "tensor_tensor", psc[:], psc[:],
                                    bm[:], ALU.min,
                                    waits=[("pe", P.cur("pe"))])
                            ex_tok = sc(P, "activation", PT[:], psc[:],
                                        AF.Exp, 0.0, 0.08838834764831845,
                                        waits=[("ve", mk)])
                            vtile = val[:, i * 1024 + h * 128:
                                        i * 1024 + (h + 1) * 128]
                            mm(P, pyin[:, 0:128], PT[:], vtile,
                               start=True, stop=False,
                               waits=[("sc", ex_tok)])
                            mm(P, pyin[:, 128:129], PT[:], onec[:],
                               start=False, stop=True)
                            for cc in range(2):
                                c = 2 * i + cc
                                gcol = gb[:, h * 32 + c:h * 32 + c + 1]
                                Srd = Sf[:, cc * 128:(cc + 1) * 128]
                                Swr = Sf[:, (1 - cc) * 128:(2 - cc) * 128]
                                cb = ve(P, "tensor_copy", Sb[:], Srd,
                                        waits=[("pe", P.cur("pe"))])
                                mm(P, pyc[cc * 64:(cc + 1) * 64, :],
                                   qT[:, c * 64:c * 64 + 64], Sb[:],
                                   start=True, stop=True,
                                   tile_position=(0, 64) if cc else None,
                                   waits=[("ve", cb)])
                                if cc == 0:
                                    kc = hs[0:64,
                                            2048 + i * 128:
                                            2048 + i * 128 + 128]
                                    vc = val[0:64, i * 1024 + h * 128:
                                             i * 1024 + (h + 1) * 128]
                                else:
                                    kc = klo[:, i * 128:(i + 1) * 128]
                                    vc = vlo[:, i * 128:(i + 1) * 128]
                                w2 = [("ve", P.cur("ve"))]
                                if i == 0 and cc == 1:
                                    w2.append(("l1ako", ko_tok))
                                mm(P, pkv[:], kc, vc, start=True,
                                   stop=True, waits=w2)
                                ve(P, "scalar_tensor_tensor", Swr, Srd,
                                   gcol, pkv[:], ALU.mult, ALU.add,
                                   waits=[("pe", P.cur("pe"))])
                            rc = ve(P, "reciprocal", stat[:, 8:9],
                                    pyin[:, 128:129],
                                    waits=[("pe", P.cur("pe"))])
                            ve(P, "tensor_scalar", sq[:, 0:128],
                               pyin[:, 0:128], stat[:, 8:9], None,
                               ALU.mult)
                            ve(P, "tensor_add",
                               aoh[:, i * 128:(i + 1) * 128],
                               sq[:, 0:128], pyc[:],
                               waits=[("l1aao", ao_tok)]
                               if i == 0 else ())
                        ao_tok = dma(P, "sync",
                                     ao_d.ap()[:, :, hsl]
                                     .rearrange("c p n -> p c n"),
                                     aoh[:].rearrange("p (c n) -> p c n",
                                                      n=128),
                                     waits=[("ve", P.cur("ve"))],
                                     sem="l1aao")
                    P.barrier()

                with (nc.sbuf_tensor(nm("l1a_wo"), [128, 8 * 1024],
                                     BF16) as wo,
                      nc.psum_tensor(nm("l1a_py"), [128, 1024], F32) as py,
                      nc.psum_tensor(nm("l1a_pt3"), [128, 1024],
                                     BF16) as pt3):
                    emit_load_w(P, env, "l1a_WoT", wo, 8, 1024)
                    P.barrier(engines=("tensor", "sync"), sems=("sdg",))
                    ai_tok = 0
                    for i in range(NT):
                        ai_tok = dma(P, "sync", xn[:],
                                     ao_d.ap()[i],
                                     waits=[("pe", P.cur("pe"))],
                                     sem="l1aai")
                        tt = emit_transpose_tile(P, xn[:], xnT[:], pt3,
                                                 ident, P.cur("ve"),
                                                 wait_extra=[("l1aai",
                                                              ai_tok)])
                        for g in range(2):
                            pv = emit_proj(
                                P, py[:, g * 512:(g + 1) * 512],
                                xnT, wo, 1024, g * 512, 512,
                                wait_first=[("ve", P.cur("ve"))])
                        rtile = res[:, (t0 + i) * 1024:(t0 + i + 1) * 1024]
                        ve(P, "tensor_add", rtile[:, 0:512],
                           rtile[:, 0:512], py[:, 0:512],
                           waits=[("pe", pv)])
                        ve(P, "tensor_add", rtile[:, 512:1024],
                           rtile[:, 512:1024], py[:, 512:1024])
                    P.barrier()
        P.barrier()


def prep_l3_attn(inp, c):
    D, DI, H, PH, N = 1024, 2048, 32, 64, 16
    w_in = _f32(inp["L3_attn_w_in"])
    n1 = _f32(inp["L3_attn_norm1_w"]).reshape(D)
    WinT = np.zeros((D, 4352), np.float32)
    wf = (w_in * n1[None, :]).T
    WinT[:, 0:2048] = wf[:, 0:2048]
    WinT[:, 2048:4096] = wf[:, 2048:4096]
    pr = wf[:, 4096:4160].reshape(D, 32, 2)
    WinT[:, 4096:4128] = pr[:, :, 1]
    WinT[:, 4128:4160] = pr[:, :, 0]
    WinT[:, 4224:4288] = wf[:, 4160:4224]
    c["l3a_WinT"] = WinT
    cw = _f32(inp["L3_attn_conv_w"])
    c["l3a_cw"] = np.concatenate([fm_layout(cw[:, k]) for k in range(4)],
                                 axis=1)
    c["l3a_cb"] = fm_layout(_f32(inp["L3_attn_conv_b"]))
    onw = _f32(inp["L3_attn_out_norm_w"]).reshape(DI)
    c["l3a_WoutT"] = _f32(_f32(inp["L3_attn_w_out"]).T * onw[:, None])
    n2 = _f32(inp["L3_attn_norm2_w"]).reshape(D)
    c["l3a_W1T"] = _f32(_f32(inp["L3_attn_ff_w1"]).T * n2[:, None])
    c["l3a_b1"] = _f32(inp["L3_attn_ff_b1"]).reshape(1, 4096)
    c["l3a_W2T"] = _f32(np.ascontiguousarray(_f32(inp["L3_attn_ff_w2"]).T))
    c["l3a_b2"] = _f32(inp["L3_attn_ff_b2"]).reshape(1, 1024)
    if "C_l3a_tri" not in c:
        j = np.arange(128)
        c["C_l3a_tri"] = _bf((j[:, None] <= j[None, :]).astype(np.float32))
        m2 = np.zeros((32, 16), np.float32)
        m2[np.arange(16), np.arange(16)] = 1.0
        m2[16 + np.arange(16), np.arange(16)] = 1.0
        c["C_l3a_mask2"] = m2
        s2 = np.zeros((2, 32), np.float32)
        s2[0, 0:16] = 1.0
        s2[1, 16:32] = 1.0
        c["C_l3a_sel2"] = s2
        sgn = np.ones((128, 1), np.float32)
        for hh in range(4):
            sgn[hh * 32 + 16:hh * 32 + 32] = -1.0
        c["C_l3a_sgn"] = sgn
        selP = np.zeros((32, 8, 2, 128), np.float32)
        for hg in range(8):
            for q in range(128):
                hh, m = q // 32, (q % 32) // 16
                selP[hg * 4 + hh, hg, m, q] = 1.0
        c["C_l3a_selP"] = selP.reshape(32, 2048)
        selBC = np.zeros((64, 4, 128), np.float32)
        for jr in range(4):
            for q in range(128):
                n = q % 16
                selBC[jr * 16 + n, jr, q] = 1.0
        c["C_l3a_selBC"] = selBC.reshape(64, 512)


def emit_l3_attn(P, env):
    nc, aps = env.nc, env.aps
    res = env.sb["res"]
    ident = env.sb["ident"]
    identf = env.sb["identf"]
    ones1 = env.sb["ones1"]
    g_d = env.scratch("l3a_g_d", [16, 128, 2048], BF16)
    y_d = env.scratch("l3a_y_d", [16, 128, 2048], BF16)
    h1_d = env.scratch("l3a_h1_d", [16, 128, 4096], BF16)
    sav_d = env.scratch("l3a_sav_d", [32, 128, 1024], BF16)
    for sname in ("l3asv", "l3agd", "l3apl", "l3ach", "l3ay", "l3ay2",
                  "l3agl", "l3ah1", "l3acn"):
        P.sem(sname)

    with ExitStack() as ctx:
        e = ctx.enter_context
        xn = e(nc.sbuf_tensor(nm("l3a_xn"), [128, 1024], BF16))
        sq = e(nc.sbuf_tensor(nm("l3a_sq"), [128, 1024], F32))
        stat = e(nc.sbuf_tensor(nm("l3a_st"), [128, 16], F32))
        cw = e(nc.sbuf_tensor(nm("l3a_cw"), [128, 64], F32))
        cb = e(nc.sbuf_tensor(nm("l3a_cb"), [128, 16], F32))
        tri = e(nc.sbuf_tensor(nm("l3a_tri"), [128, 128], BF16))
        mask2 = e(nc.sbuf_tensor(nm("l3a_m2"), [32, 16], F32))
        sel2 = e(nc.sbuf_tensor(nm("l3a_s2"), [2, 32], F32))
        sgn = e(nc.sbuf_tensor(nm("l3a_sg"), [128, 1], F32))
        Sall = e(nc.sbuf_tensor(nm("l3a_Sa"), [32, 2048], F32))
        halo3 = e(nc.sbuf_tensor(nm("l3a_h3"), [128, 48], BF16))
        PLe = e(nc.sbuf_tensor(nm("l3a_ple"), [32, 32], F32))
        R2 = e(nc.sbuf_tensor(nm("l3a_r2"), [2, 16], F32))
        Vt = e(nc.sbuf_tensor(nm("l3a_vt"), [32, 16], F32))
        PLM = e(nc.sbuf_tensor(nm("l3a_plm"), [32, 32], F32))
        Ssum = e(nc.sbuf_tensor(nm("l3a_ssu"), [32, 64], F32))
        WTm = e(nc.sbuf_tensor(nm("l3a_wtm"), [128, 128], BF16))
        yh = e(nc.sbuf_tensor(nm("l3a_yh"), [128, 512], BF16))
        P.barrier()
        dma(P, "gpsimd", cw[:], aps["l3a_cw"])
        dma(P, "gpsimd", cb[:], aps["l3a_cb"])
        dma(P, "gpsimd", tri[:], aps["C_l3a_tri"])
        dma(P, "gpsimd", mask2[:], aps["C_l3a_mask2"])
        dma(P, "gpsimd", sel2[:], aps["C_l3a_sel2"])
        dma(P, "gpsimd", sgn[:], aps["C_l3a_sgn"])

        sv_tok = 0
        for gi in range(32):
            rtile = res[:, gi * 1024:(gi + 1) * 1024]
            sv_tok = dma(P, "sync", sav_d.ap()[gi], rtile, sem="l3asv",
                         waits=[("ve", P.cur("ve"))])
            emit_ln_stats(P, rtile, sq[:], stat, 0,
                          waits=[("sc", P.cur("sc"))])
            emit_ln_apply(P, rtile, stat, 0, rtile,
                          waits=[("l3asv", sv_tok)])
        P.barrier()

        for s in range(SEQS):
            t0 = s * NT
            ve(P, "memset", Sall[:], 0.0)
            for half in range(2):
                hb = half * 8
                with ExitStack() as hctx:
                    he = hctx.enter_context
                    U_tm = he(nc.sbuf_tensor(nm("l3a_utm"),
                                             [128, 8 * 2048], BF16))
                    dtfm = he(nc.sbuf_tensor(nm("l3a_dt"), [32, 1024],
                                             F32))
                    dt2fm = he(nc.sbuf_tensor(nm("l3a_dt2"), [32, 1024],
                                              F32))
                    Afm = he(nc.sbuf_tensor(nm("l3a_af"), [32, 1024],
                                            F32))
                    bcfm = he(nc.sbuf_tensor(nm("l3a_bc"), [64, 1024],
                                             F32))
                    _l3a_p1_conv(P, env, s, half, U_tm, dtfm, dt2fm, Afm,
                                 bcfm, halo3, xn, sq, stat, cw, cb)
                    with ExitStack() as fctx:
                        Parr = fctx.enter_context(
                            nc.sbuf_tensor(nm("l3a_pa"), [32, 4096], F32))
                        _l3a_prefix(P, env, Parr, dtfm, dt2fm, Afm, PLe)
                        _l3a_scan(P, env, s, half, U_tm, dtfm, dt2fm,
                                  bcfm, Parr, PLe, R2, Vt, PLM, Ssum,
                                  WTm, yh, Sall, tri, mask2, sel2, sgn,
                                  y_d)
                    _l3a_pha(P, env, s, half, g_d, y_d, xn, sq, stat)
            _l3a_ffn(P, env, s, h1_d, sav_d, xn, sq, stat)
        P.barrier()


def _l3a_p1_conv(P, env, s, half, U_tm, dtfm, dt2fm, Afm, bcfm, halo3,
                 xn, sq, stat, cw, cb):
    nc, aps = env.nc, env.aps
    res = env.sb["res"]
    ident = env.sb["ident"]
    t0 = s * NT
    hb = half * 8
    with ExitStack() as ctx:
        e = ctx.enter_context
        vfm = e(nc.sbuf_tensor(nm("l3a_vf"), [128, 16 * 1028], BF16))
        prfm = e(nc.sbuf_tensor(nm("l3a_pr"), [64, 1024], F32))
        vfm3 = vfm[:].rearrange("p (c n) -> p c n", n=1028)
        if half == 0:
            ve(P, "memset", vfm3[:, :, 0:4], 0.0)
        else:
            ve(P, "tensor_copy", vfm3[:, :, 1:4],
               halo3[:].rearrange("p (c n) -> p c n", n=3))
        for pas in range(4):
            with ExitStack() as wctx:
                we = wctx.enter_context
                if pas < 2:
                    wsb = we(nc.sbuf_tensor(nm("l3a_wg"),
                                            [128, 8 * 1024], BF16))
                    emit_load_w(P, env, "", wsb, 8, 1024,
                                src_ap=aps["l3a_WinT"][:, pas * 1024:
                                                       (pas + 1) * 1024])
                    gt = we(nc.sbuf_tensor(nm("l3a_gt"), [128, 1024],
                                           BF16))
                else:
                    nfc = 9
                    fb = (pas - 2) * 9
                    wsb = we(nc.sbuf_tensor(nm("l3a_wf"),
                                            [128, 8 * 1152], BF16))
                    emit_load_w(P, env, "", wsb, 8, 1152,
                                src_ap=aps["l3a_WinT"]
                                [:, 2048 + fb * 128:
                                 2048 + (fb + nfc) * 128])
                xnT4 = we(nc.sbuf_tensor(nm("l3a_x4"), [128, 8 * 512],
                                         BF16))
                x43 = xnT4[:].rearrange("p (c n) -> p c n", n=512)
                pg = we(nc.psum_tensor(nm("l3a_pg"), [128, 2048], F32))
                ptp = we(nc.psum_tensor(nm("l3a_ptp"), [128, 1024],
                                        BF16))
                P.barrier(engines=("tensor",), sems=("sdg",))
                gd_tok = 0
                for g in range(2):
                    tts = []
                    for j in range(4):
                        ti = hb + g * 4 + j
                        rtile = res[:, (t0 + ti) * 1024:
                                    (t0 + ti + 1) * 1024]
                        emit_ln_stats(P, rtile, sq[:], stat, 0, rms=True,
                                      eps=1e-6,
                                      waits=[("sc", P.cur("sc"))])
                        lt = emit_ln_apply(P, rtile, stat, 0, xn[:])
                        tts.append(emit_transpose_tile(
                            P, xn[:], x43[:, :, j * 128:(j + 1) * 128],
                            ptp, ident, lt,
                            wait_extra=[("pe", P.cur("pe"))]))
                    if pas < 2:
                        for j in range(4):
                            ti = hb + g * 4 + j
                            for cg in range(2):
                                for k in range(8):
                                    mm(P, pg[:, cg * 512:(cg + 1) * 512],
                                       x43[:, k, j * 128:(j + 1) * 128],
                                       wsb[:, k * 1024 + cg * 512:
                                           k * 1024 + (cg + 1) * 512],
                                       start=(k == 0), stop=(k == 7),
                                       waits=[("ve", tts[j]),
                                              ("l3agd", gd_tok),
                                              ("ve", P.cur("ve"))]
                                       if k == 0 else ())
                            ve(P, "tensor_copy", gt[:], pg[:, 0:1024],
                               waits=[("pe", P.cur("pe"))])
                            gd_tok = dma(P, "sync",
                                         g_d_ap(env)[ti][:,
                                                         pas * 1024:
                                                         (pas + 1)
                                                         * 1024],
                                         gt[:],
                                         waits=[("ve", P.cur("ve"))],
                                         sem="l3agd")
                    else:
                        for fcl in range(nfc):
                            fc = fb + fcl
                            for k in range(8):
                                mm(P, pg[:, 0:512],
                                   wsb[:, k * 1152 + fcl * 128:
                                       k * 1152 + (fcl + 1) * 128],
                                   x43[:, k, :],
                                   start=(k == 0), stop=(k == 7),
                                   waits=[("ve", P.cur("ve"))]
                                   if k == 0 else ())
                            w = [("pe", P.cur("pe"))]
                            if fc < 16:
                                ve(P, "tensor_copy",
                                   vfm3[:, fc, 4 + g * 512:
                                        4 + (g + 1) * 512],
                                   pg[0:128, 0:512], waits=w)
                            elif fc == 16:
                                ve(P, "tensor_copy",
                                   prfm[0:64, g * 512:(g + 1) * 512],
                                   pg[0:64, 0:512], waits=w)
                            else:
                                ve(P, "tensor_copy",
                                   bcfm[0:64, g * 512:(g + 1) * 512],
                                   pg[0:64, 0:512], waits=w)
                P.barrier()
        with ExitStack() as cctx:
            ce = cctx.enter_context
            acc = ce(nc.sbuf_tensor(nm("l3a_ac"), [128, 1024], F32))
            vtmp = ce(nc.sbuf_tensor(nm("l3a_vt2"), [128, 1024], F32))
            vfb = ce(nc.sbuf_tensor(nm("l3a_vb"), [128, 1024], BF16))
            ptp = ce(nc.psum_tensor(nm("l3a_pt2"), [128, 1024], BF16))
            ut3 = U_tm[:].rearrange("p (c n) -> p c n", n=2048)
            for fc in range(16):
                ve(P, "tensor_scalar", acc[:], vfm3[:, fc, 1:1025],
                   cw[:, 0 * 16 + fc:0 * 16 + fc + 1], None, ALU.mult,
                   waits=[("pe", P.cur("pe"))])
                for k in range(1, 4):
                    ve(P, "scalar_tensor_tensor", acc[:],
                       vfm3[:, fc, 1 + k:1 + k + 1024],
                       cw[:, k * 16 + fc:k * 16 + fc + 1], acc[:],
                       ALU.mult, ALU.add)
                ve(P, "tensor_scalar", vtmp[:], acc[:],
                   cb[:, fc:fc + 1], None, ALU.add)
                sc(P, "activation", acc[:], vtmp[:], AF.Sigmoid,
                   waits=[("ve", P.cur("ve"))])
                ve(P, "tensor_tensor", vfb[:], vtmp[:], acc[:], ALU.mult,
                   waits=[("sc", P.cur("sc"))])
                for tt in range(8):
                    tr(P, ptp[:, tt * 128:(tt + 1) * 128],
                       vfb[:, tt * 128:(tt + 1) * 128], ident,
                       waits=[("ve", P.cur("ve"))] if tt == 0 else ())
                ve(P, "tensor_copy", ut3[:, :, fc * 128:(fc + 1) * 128],
                   ptp[:].rearrange("p (c n) -> p c n", n=128),
                   waits=[("pe", P.cur("pe"))])
            ve(P, "tensor_copy", halo3[:].rearrange("p (c n) -> p c n",
                                                    n=3),
               vfm3[:, :, 1025:1028])
        ad_tok = dma(P, "sync", Afm[:], prfm[32:64, :], sem="l3acn",
                     waits=[("ve", P.cur("ve"))])
        sc(P, "activation", sq[0:32, 0:1024], Afm[:], AF.Abs,
           waits=[("l3acn", ad_tok), ("sc", P.cur("sc"))])
        sc(P, "activation", dt2fm[:], sq[0:32, 0:1024], AF.Exp, 0.0,
           -1.0, waits=[("sc", P.cur("sc"))])
        ve(P, "tensor_scalar", dt2fm[:], dt2fm[:], 1.0, None, ALU.add,
           waits=[("sc", P.cur("sc")), ("ve", P.cur("ve"))])
        sc(P, "activation", dt2fm[:], dt2fm[:], AF.Ln,
           waits=[("ve", P.cur("ve"))])
        ve(P, "tensor_scalar_max", sq[0:32, 0:1024], Afm[:], 0.0,
           waits=[("sc", P.cur("sc"))])
        ve(P, "tensor_tensor", Afm[:], dt2fm[:], sq[0:32, 0:1024],
           ALU.add)
        sc(P, "activation", dtfm[:], prfm[0:32, :], AF.Sigmoid,
           waits=[("ve", P.cur("ve"))])
        ve(P, "tensor_tensor", dt2fm[:], dtfm[:], dtfm[:], ALU.mult,
           waits=[("sc", P.cur("sc"))])
        P.barrier()


def g_d_ap(env):
    return env.scratch("l3a_g_d", [16, 128, 2048], BF16).ap()


def _l3a_prefix(P, env, Parr, dtfm, dt2fm, Afm, PLe):
    nc = env.nc

    def pv(buf, ent, off, width):
        return (buf[:, ent * 1024:(ent + 1) * 1024]
                .rearrange("p (c n) -> p c n", n=128)[:, :, off:off + width])

    with ExitStack() as ctx:
        e = ctx.enter_context
        ping = e(nc.sbuf_tensor(nm("l3a_pb"), [32, 4096], F32))
        w1 = e(nc.sbuf_tensor(nm("l3a_w1t"), [32, 1024], F32))
        w2 = e(nc.sbuf_tensor(nm("l3a_w2t"), [32, 1024], F32))
        ve(P, "memset", Parr[:, 0:1024], 1.0, waits=[("sc", P.cur("sc"))])
        ve(P, "tensor_tensor", Parr[:, 1024:2048], dtfm[:], Afm[:],
           ALU.mult)
        ve(P, "tensor_scalar", Parr[:, 1024:2048], Parr[:, 1024:2048],
           -1.0, None, ALU.mult)
        ve(P, "tensor_copy", Parr[:, 2048:3072], dtfm[:])
        ve(P, "tensor_tensor", Parr[:, 3072:4096], dt2fm[:], Afm[:],
           ALU.mult)
        ve(P, "tensor_scalar", Parr[:, 3072:4096], Parr[:, 3072:4096],
           -1.0, 1.0, ALU.mult, ALU.add)
        cur, oth = Parr, ping
        for lv in range(7):
            sh = 1 << lv
            wd = 128 - sh
            for (oe, a1, b1, a2, b2) in ((0, 0, 0, 1, 2), (1, 0, 1, 1, 3),
                                         (2, 2, 0, 3, 2), (3, 2, 1, 3, 3)):
                ve(P, "tensor_tensor", pv(w1, 0, sh, wd),
                   pv(cur, a1, sh, wd), pv(cur, b1, 0, wd), ALU.mult)
                ve(P, "tensor_tensor", pv(w2, 0, sh, wd),
                   pv(cur, a2, sh, wd), pv(cur, b2, 0, wd), ALU.mult)
                ve(P, "tensor_tensor", pv(oth, oe, sh, wd),
                   pv(w1, 0, sh, wd), pv(w2, 0, sh, wd), ALU.add)
                ve(P, "tensor_copy", pv(oth, oe, 0, sh),
                   pv(cur, oe, 0, sh))
            cur, oth = oth, cur
        if cur is not Parr:
            for ent in range(4):
                ve(P, "tensor_copy",
                   Parr[:, ent * 1024:(ent + 1) * 1024],
                   ping[:, ent * 1024:(ent + 1) * 1024])
        for (base, j, ent) in ((0, 0, 0), (0, 1, 2), (16, 0, 1),
                               (16, 1, 3)):
            dst = (PLe[:, base:base + 16]
                   .rearrange("p (c j) -> p c j", j=2)[:, :, j:j + 1])
            ve(P, "tensor_copy", dst, pv(Parr, ent, 127, 1))


def _l3a_scan(P, env, s, half, U_tm, dtfm, dt2fm, bcfm, Parr, PLe, R2,
              Vt, PLM, Ssum, WTm, yh, Sall, tri, mask2, sel2, sgn, y_d):
    nc, aps = env.nc, env.aps
    identf = env.sb["identf"]
    ut3 = U_tm[:].rearrange("p (c n) -> p c n", n=2048)
    with ExitStack() as ctx:
        e = ctx.enter_context
        selP = e(nc.sbuf_tensor(nm("l3a_sp"), [32, 2048], F32))
        selBC = e(nc.sbuf_tensor(nm("l3a_sb2"), [64, 512], F32))
        reps = e(nc.sbuf_tensor(nm("l3a_rp"), [128, 4 * 1024], BF16))
        KB0 = e(nc.sbuf_tensor(nm("l3a_kb0"), [128, 1024], BF16))
        KB1 = e(nc.sbuf_tensor(nm("l3a_kb1"), [128, 1024], BF16))
        Cpk = e(nc.sbuf_tensor(nm("l3a_cp"), [128, 1024], F32))
        Bpk = e(nc.sbuf_tensor(nm("l3a_bp"), [128, 1024], F32))
        tm1 = e(nc.sbuf_tensor(nm("l3a_t1"), [128, 1024], F32))
        tm2 = e(nc.sbuf_tensor(nm("l3a_t2"), [128, 1024], F32))
        BT4 = e(nc.sbuf_tensor(nm("l3a_bt4"), [128, 8 * 128], F32))
        Uf = e(nc.sbuf_tensor(nm("l3a_uf"), [128, 8 * 256], F32))
        Ch = e(nc.sbuf_tensor(nm("l3a_ch"), [32, 1024], F32))
        Bh = e(nc.sbuf_tensor(nm("l3a_bh"), [32, 1024], F32))
        dma(P, "gpsimd", selP[:], aps["C_l3a_selP"],
            waits=[("ve", P.cur("ve")), ("pe", P.cur("pe"))])
        dma(P, "gpsimd", selBC[:], aps["C_l3a_selBC"])
        with (nc.psum_tensor(nm("l3a_px"), [128, 1024], F32) as pX,
              nc.psum_tensor(nm("l3a_py2"), [128, 1024], F32) as pY):
            P.barrier(engines=("tensor",), sems=("sdg", "ve"))
            for jr in range(4):
                pp = pX if jr < 2 else pY
                for cg in range(2):
                    mm(P, pp[:, cg * 512:(cg + 1) * 512],
                       selBC[:, jr * 128:(jr + 1) * 128],
                       bcfm[0:64, cg * 512:(cg + 1) * 512],
                       start=True, stop=True,
                       waits=[("ve", P.cur("ve"))] if cg == 0 else ())
                ve(P, "tensor_copy",
                   reps[:, jr * 1024:(jr + 1) * 1024], pp[:],
                   waits=[("pe", P.cur("pe"))])
            P.barrier()
        if True:
            ch_tok = 0
            pl_tok = 0
            ya_tok = 0
            for hg in range(8):
                with (nc.psum_tensor(nm("l3a_px"), [128, 1024],
                                     F32) as pX,
                      nc.psum_tensor(nm("l3a_py2"), [128, 1024],
                                     F32) as pY):
                  sm0 = selP[:, (hg * 2 + 0) * 128:(hg * 2 + 1) * 128]
                  sm1 = selP[:, (hg * 2 + 1) * 128:(hg * 2 + 2) * 128]
                  for (pp, src, kb, brep) in (
                          (pX, dtfm, KB0, reps[:, 0:1024]),
                          (pY, dt2fm, KB1, reps[:, 1024:2048])):
                      for cg in range(2):
                          mm(P, pp[:, cg * 512:(cg + 1) * 512], sm0,
                             src[:, cg * 512:(cg + 1) * 512],
                             start=True, stop=False,
                             waits=[("ve", P.cur("ve"))] if cg == 0 else ())
                          mm(P, pp[:, cg * 512:(cg + 1) * 512], sm1,
                             src[:, cg * 512:(cg + 1) * 512],
                             start=False, stop=True)
                      ve(P, "tensor_tensor", kb, pp[:], brep, ALU.mult,
                         waits=[("pe", P.cur("pe"))])
                  for (pp, e0, e1) in ((pX, 3, 2), (pY, 1, 0)):
                      for cg in range(2):
                          mm(P, pp[:, cg * 512:(cg + 1) * 512], sm0,
                             Parr[:, e0 * 1024 + cg * 512:
                                  e0 * 1024 + (cg + 1) * 512],
                             start=True, stop=False,
                             waits=[("ve", P.cur("ve"))] if cg == 0 else ())
                          mm(P, pp[:, cg * 512:(cg + 1) * 512], sm1,
                             Parr[:, e1 * 1024 + cg * 512:
                                  e1 * 1024 + (cg + 1) * 512],
                             start=False, stop=True)
                  ve(P, "tensor_tensor", tm1[:], pX[:], KB0[:], ALU.mult,
                     waits=[("pe", P.cur("pe"))])
                  ve(P, "tensor_tensor", tm2[:], pY[:], KB1[:], ALU.mult)
                  ve(P, "tensor_sub", tm1[:], tm1[:], tm2[:])
                  ve(P, "tensor_scalar", Bpk[:], tm1[:], sgn[:, 0:1], None,
                     ALU.mult)
                  for (pp, e0, e1) in ((pX, 0, 1), (pY, 2, 3)):
                      for cg in range(2):
                          mm(P, pp[:, cg * 512:(cg + 1) * 512], sm0,
                             Parr[:, e0 * 1024 + cg * 512:
                                  e0 * 1024 + (cg + 1) * 512],
                             start=True, stop=False,
                             waits=[("ve", P.cur("ve"))] if cg == 0 else ())
                          mm(P, pp[:, cg * 512:(cg + 1) * 512], sm1,
                             Parr[:, e1 * 1024 + cg * 512:
                                  e1 * 1024 + (cg + 1) * 512],
                             start=False, stop=True)
                  ve(P, "tensor_tensor", tm1[:], pX[:], reps[:, 2048:3072],
                     ALU.mult, waits=[("pe", P.cur("pe"))])
                  ve(P, "tensor_tensor", tm2[:], pY[:], reps[:, 3072:4096],
                     ALU.mult)
                  ve(P, "tensor_add", Cpk[:], tm1[:], tm2[:])
                  ve(P, "tensor_copy",
                     Uf[:].rearrange("p (c n) -> p c n", n=256),
                     ut3[:, :, hg * 256:(hg + 1) * 256])
                  uf_tok = P.cur("ve")
                  P.barrier(engines=("tensor",), sems=("ve",))
                with nc.psum_tensor(nm("l3a_ptf"), [128, 128],
                                    F32) as ptf:
                    for t in range(8):
                        tr(P, ptf[:], Bpk[:, t * 128:(t + 1) * 128],
                           identf, waits=[("ve", P.cur("ve"))])
                        ve(P, "tensor_copy",
                           BT4[:, t * 128:(t + 1) * 128], ptf[:],
                           waits=[("pe", P.cur("pe"))])
                bt_tok = P.cur("ve")
                with (nc.psum_tensor(nm("l3a_pw"), [128, 128], F32)
                      as pW,
                      nc.psum_tensor(nm("l3a_pyy"), [128, 64], F32)
                      as pYY,
                      nc.psum_tensor(nm("l3a_py0"), [128, 64], F32)
                      as pY0,
                      nc.psum_tensor(nm("l3a_ptt"), [32, 64], F32)
                      as pTT,
                      nc.psum_tensor(nm("l3a_prt"), [32, 64], F32)
                      as pRT,
                      nc.sbuf_tensor(nm("l3a_y0b"), [128, 64], F32)
                      as y0b):
                    for hh in range(4):
                        h = hg * 4 + hh
                        Sh = Sall[:, h * 64:(h + 1) * 64]
                        ch_tok = dma(P, "sync", Ch[:],
                                     Cpk[hh * 32:(hh + 1) * 32, :],
                                     waits=[("ve", bt_tok),
                                            ("pe", P.cur("pe"))],
                                     sem="l3ach")
                        ch_tok = dma(P, "sync", Bh[:],
                                     Bpk[hh * 32:(hh + 1) * 32, :],
                                     sem="l3ach")
                        pl_tok = dma(P, "gpsimd",
                                     R2[:],
                                     PLe[h:h + 1, :]
                                     .rearrange("o (p n) -> o p n", p=2),
                                     waits=[("pe", P.cur("pe"))],
                                     sem="l3apl")
                        mm(P, pTT[0:32, 0:16], sel2[:], R2[:],
                           start=True, stop=True,
                           waits=[("l3apl", pl_tok)])
                        ve(P, "tensor_copy", Vt[:], pTT[0:32, 0:16],
                           waits=[("pe", P.cur("pe"))])
                        for t in range(8):
                            mm(P, pW[:], Bh[:, t * 128:(t + 1) * 128],
                               Ch[:, t * 128:(t + 1) * 128],
                               start=True, stop=True,
                               waits=[("l3ach", ch_tok),
                                      ("ve", P.cur("ve"))])
                            ve(P, "tensor_tensor", WTm[:], pW[:], tri[:],
                               ALU.mult, waits=[("pe", P.cur("pe"))])
                            mm(P, pYY[:], WTm[:],
                               ut3[:, t, h * 64:(h + 1) * 64],
                               start=True, stop=True,
                               waits=[("ve", P.cur("ve"))])
                            mm(P, pY0[:], Ch[:, t * 128:(t + 1) * 128],
                               Sh, start=True, stop=True)
                            mm(P, pTT[:],
                               BT4[:, t * 128 + hh * 32:
                                   t * 128 + (hh + 1) * 32],
                               Uf[:, t * 256 + hh * 64:
                                  t * 256 + (hh + 1) * 64],
                               start=True, stop=True)
                            ve(P, "tensor_copy", y0b[:], pY0[:],
                               waits=[("pe", P.cur("pe")),
                                      ("l3ay", ya_tok)])
                            ve(P, "tensor_tensor",
                               yh[:, t * 64:(t + 1) * 64], y0b[:],
                               pYY[:], ALU.add)
                            ve(P, "tensor_tensor", Ssum[:], Sh, pTT[:],
                               ALU.add)
                            c2 = 2 * t
                            ve(P, "tensor_scalar", PLM[:, 0:16],
                               mask2[:], Vt[:, c2:c2 + 1], None,
                               ALU.mult)
                            ve(P, "tensor_scalar", PLM[:, 16:32],
                               mask2[:], Vt[:, c2 + 1:c2 + 2], None,
                               ALU.mult)
                            mm(P, pRT[:], PLM[:], Ssum[:], start=True,
                               stop=True, waits=[("ve", P.cur("ve"))])
                            ve(P, "tensor_copy", Sh, pRT[:],
                               waits=[("pe", P.cur("pe"))])
                        ya_tok = dma(P, "sync",
                                     y_d.ap()[half * 8:half * 8 + 8, :,
                                              h * 64:(h + 1) * 64]
                                     .rearrange("c p n -> p c n"),
                                     yh[:].rearrange("p (c n) -> p c n",
                                                     n=64),
                                     waits=[("ve", P.cur("ve"))],
                                     sem="l3ay")
                P.barrier(engines=("tensor", "vector"),
                          sems=("pe", "ve", "l3ay"))
        P.barrier()


def _l3a_pha(P, env, s, half, g_d, y_d, xn, sq, stat):
    nc, aps = env.nc, env.aps
    res = env.sb["res"]
    ident = env.sb["ident"]
    t0 = s * NT
    with ExitStack() as ctx:
        e = ctx.enter_context
        wo = e(nc.sbuf_tensor(nm("l3a_wo"), [128, 16 * 1024], BF16))
        gt = e(nc.sbuf_tensor(nm("l3a_gt2"), [128, 2048], BF16))
        ysb = e(nc.sbuf_tensor(nm("l3a_ys"), [128, 2048], BF16))
        sgb = e(nc.sbuf_tensor(nm("l3a_sgb"), [128, 2048], BF16))
        yf = e(nc.sbuf_tensor(nm("l3a_yf"), [128, 2048], F32))
        sqf = e(nc.sbuf_tensor(nm("l3a_sqf"), [128, 2048], F32))
        ygn = e(nc.sbuf_tensor(nm("l3a_ygn"), [128, 2048], BF16))
        ygnT = e(nc.sbuf_tensor(nm("l3a_ygt"), [128, 2048], BF16))
        ppj = e(nc.psum_tensor(nm("l3a_ppj"), [128, 1024], F32))
        ptp = e(nc.psum_tensor(nm("l3a_pt3"), [128, 1024], BF16))
        emit_load_w(P, env, "l3a_WoutT", wo, 16, 1024)
        P.barrier(engines=("tensor", "vector"), sems=("sdg",))
        gl_tok = 0
        y2_tok = 0
        for ti in range(half * 8, half * 8 + 8):
            gl_tok = dma(P, "sync", gt[:], g_d.ap()[ti],
                         waits=[("ve", P.cur("ve"))], sem="l3agl")
            y2_tok = dma(P, "sync", ysb[:], y_d.ap()[ti],
                         waits=[("ve", P.cur("ve"))], sem="l3ay2")
            sc(P, "activation", sgb[:], gt[:], AF.Sigmoid,
               waits=[("l3agl", gl_tok), ("sc", P.cur("sc"))])
            ve(P, "tensor_tensor", sgb[:], gt[:], sgb[:], ALU.mult,
               waits=[("sc", P.cur("sc")), ("ve", P.cur("ve"))])
            ve(P, "tensor_tensor", yf[:], ysb[:], sgb[:], ALU.mult,
               waits=[("l3ay2", y2_tok)])
            sc(P, "activation", sqf[:], yf[:], AF.Square,
               waits=[("ve", P.cur("ve"))])
            ve(P, "reduce_sum", stat[:, 0:1], sqf[:], axis=AX.X,
               waits=[("sc", P.cur("sc"))])
            ve(P, "tensor_scalar", stat[:, 0:1], stat[:, 0:1],
               1.0 / 2048.0, 1e-6, ALU.mult, ALU.add)
            sc(P, "activation", stat[:, 0:1], stat[:, 0:1], AF.Sqrt,
               waits=[("ve", P.cur("ve"))])
            ve(P, "reciprocal", stat[:, 0:1], stat[:, 0:1],
               waits=[("sc", P.cur("sc"))])
            ve(P, "tensor_scalar", ygn[:], yf[:], stat[:, 0:1], None,
               ALU.mult)
            for w2 in range(2):
                emit_transpose_tile(P, ygn[:, w2 * 1024:(w2 + 1) * 1024],
                                    ygnT[:, w2 * 1024:(w2 + 1) * 1024],
                                    ptp, ident, P.cur("ve"))
            rtile = res[:, (t0 + ti) * 1024:(t0 + ti + 1) * 1024]
            for g in range(2):
                pv2 = emit_proj(P, ppj[:, g * 512:(g + 1) * 512], ygnT,
                                wo, 1024, g * 512, 512, kch=16,
                                wait_first=[("ve", P.cur("ve"))])
            ve(P, "tensor_add", rtile[:, 0:512], rtile[:, 0:512],
               ppj[:, 0:512], waits=[("pe", pv2)])
            ve(P, "tensor_add", rtile[:, 512:1024], rtile[:, 512:1024],
               ppj[:, 512:1024])
        P.barrier()


def _l3a_ffn(P, env, s, h1_d, sav_d, xn, sq, stat):
    nc, aps = env.nc, env.aps
    res = env.sb["res"]
    ident = env.sb["ident"]
    ones1 = env.sb["ones1"]
    t0 = s * NT
    cgl = env.constcol(0.044715)
    with ExitStack() as ctx:
        e = ctx.enter_context
        w1 = e(nc.sbuf_tensor(nm("l3a_w1"), [128, 8 * 4096], BF16))
        b1r = e(nc.sbuf_tensor(nm("l3a_b1"), [1, 4096], BF16))
        xnT = e(nc.sbuf_tensor(nm("l3a_xt"), [128, 1024], BF16))
        h1 = e(nc.sbuf_tensor(nm("l3a_h1"), [128, 4096], BF16))
        gt1 = e(nc.sbuf_tensor(nm("l3a_g1"), [128, 512], F32))
        gt2 = e(nc.sbuf_tensor(nm("l3a_g2"), [128, 512], F32))
        gt3 = e(nc.sbuf_tensor(nm("l3a_g3"), [128, 512], F32))
        pb = e(nc.psum_tensor(nm("l3a_pb2"), [128, 512], F32))
        ptp = e(nc.psum_tensor(nm("l3a_pt4"), [128, 1024], BF16))
        emit_load_w(P, env, "l3a_W1T", w1, 8, 4096)
        dma(P, "gpsimd", b1r[:], aps["l3a_b1"])
        P.barrier(engines=("tensor", "vector"), sems=("sdg",))
        h1_tok = 0
        for ti in range(NT):
            rtile = res[:, (t0 + ti) * 1024:(t0 + ti + 1) * 1024]
            emit_ln_stats(P, rtile, sq[:], stat, 0, rms=True, eps=1e-6,
                          waits=[("sc", P.cur("sc"))])
            lt = emit_ln_apply(P, rtile, stat, 0, xn[:],
                               waits=[("l3ah1", h1_tok)])
            tt = emit_transpose_tile(P, xn[:], xnT[:], ptp, ident, lt)
            for og in range(8):
                pv2 = emit_proj(P, pb[:], xnT, w1, 4096, og * 512, 512,
                                wait_first=[("ve", tt),
                                            ("ve", P.cur("ve"))],
                                bias_row=b1r, ones1=ones1)
                sc(P, "activation", gt1[:], pb[:], AF.Square,
                   waits=[("pe", pv2), ("sc", P.cur("sc"))])
                ve(P, "tensor_tensor", gt2[:], pb[:], gt1[:], ALU.mult,
                   waits=[("sc", P.cur("sc")), ("ve", P.cur("ve"))])
                ve(P, "scalar_tensor_tensor", gt3[:], gt2[:], cgl,
                   pb[:], ALU.mult, ALU.add)
                sc(P, "activation", gt3[:], gt3[:], AF.Sigmoid, 0.0,
                   1.5957691216057308, waits=[("ve", P.cur("ve"))])
                ve(P, "tensor_tensor", h1[:, og * 512:(og + 1) * 512],
                   pb[:], gt3[:], ALU.mult,
                   waits=[("sc", P.cur("sc"))])
            h1_tok = dma(P, "sync", h1_d.ap()[ti], h1[:],
                         waits=[("ve", P.cur("ve"))], sem="l3ah1")
        P.barrier()
    with ExitStack() as ctx:
        e = ctx.enter_context
        w2 = e(nc.sbuf_tensor(nm("l3a_w2"), [128, 32 * 1024], BF16))
        b2r = e(nc.sbuf_tensor(nm("l3a_b2"), [1, 1024], BF16))
        h1s = e(nc.sbuf_tensor(nm("l3a_h1s"), [128, 4096], BF16))
        h1T = e(nc.sbuf_tensor(nm("l3a_h1t"), [128, 4096], BF16))
        py = e(nc.psum_tensor(nm("l3a_py3"), [128, 1024], F32))
        ptp = e(nc.psum_tensor(nm("l3a_pt5"), [128, 1024], BF16))
        emit_load_w(P, env, "l3a_W2T", w2, 32, 1024)
        dma(P, "gpsimd", b2r[:], aps["l3a_b2"])
        P.barrier(engines=("tensor", "vector", "sync"), sems=("sdg",))
        hl_tok = 0
        for ti in range(NT):
            hl_tok = dma(P, "sync", h1s[:], h1_d.ap()[ti],
                         waits=[("pe", P.cur("pe"))], sem="l3ah1")
            for w4 in range(4):
                emit_transpose_tile(
                    P, h1s[:, w4 * 1024:(w4 + 1) * 1024],
                    h1T[:, w4 * 1024:(w4 + 1) * 1024], ptp, ident,
                    P.cur("ve"), wait_extra=[("l3ah1", hl_tok)])
            rtile = res[:, (t0 + ti) * 1024:(t0 + ti + 1) * 1024]
            for g in range(2):
                pv2 = emit_proj(P, py[:, g * 512:(g + 1) * 512], h1T,
                                w2, 1024, g * 512, 512, kch=32,
                                wait_first=[("ve", P.cur("ve"))],
                                bias_row=b2r, ones1=ones1)
            ve(P, "tensor_add", rtile[:, 0:512], rtile[:, 0:512],
               py[:, 0:512], waits=[("pe", pv2)])
            ve(P, "tensor_add", rtile[:, 512:1024], rtile[:, 512:1024],
               py[:, 512:1024])
        P.barrier()
    with (nc.sbuf_tensor(nm("l3a_sv"), [128, 1024], BF16) as sv):
        sv_tok = 0
        for ti in range(NT):
            gi = s * NT + ti
            sv_tok = dma(P, "sync", sv[:], sav_d.ap()[gi],
                         waits=[("ve", P.cur("ve"))], sem="l3asv")
            rtile = res[:, gi * 1024:(gi + 1) * 1024]
            ve(P, "tensor_tensor", rtile, rtile, sv[:], ALU.add,
               waits=[("l3asv", sv_tok)])
        P.barrier()


SUBLAYERS = {
    "l2_attn": (prep_l2_attn, emit_l2_attn),
    "l1_ffn": (prep_l1_ffn, emit_l1_ffn),
    "l0_attn": (prep_l0_attn, emit_l0_attn),
    "l0_ffn": (prep_l0_ffn, emit_l0_ffn),
    "l2_ffn": (prep_l2_ffn, emit_l2_ffn),
    "l3_ffn": (prep_l3_ffn, emit_l3_ffn),
    "l1_attn": (prep_l1_attn, emit_l1_attn),
    "l3_attn": (prep_l3_attn, emit_l3_attn),
}

CHAIN = [
    "l0_ffn", "l0_attn", "l0_ffn",
    "l1_ffn", "l1_attn", "l1_ffn",
    "l2_ffn", "l2_attn", "l2_ffn",
    "l3_ffn", "l3_attn", "l3_ffn",
]


def host_prep(inputs, subs):
    c = {}
    c["C_identb"] = np.eye(128, dtype=BF)
    c["C_identf"] = np.eye(128, dtype=np.float32)
    c["C_ones1"] = np.ones((1, 128), dtype=BF)
    done = set()
    for s in subs:
        if s in done or s not in SUBLAYERS:
            continue
        done.add(s)
        SUBLAYERS[s][0](inputs, c)
    return c


def build_group(subs, shapes):
    nc = bass.Bass("TRN2", target_bir_lowering=False)
    aps = {}
    for name, (shp, dt) in shapes.items():
        aps[name] = nc.dram_tensor(name, list(shp), dt,
                                   kind="ExternalInput").ap()
    xout = nc.dram_tensor("xout", [SEQS, T, D], F32, kind="ExternalOutput").ap()

    P = Prog(nc)
    env = Env(nc, P, aps)
    with P.stack:
        res = P.stack.enter_context(
            nc.sbuf_tensor("res", [128, 32 * 1024], BF16))
        ident = P.stack.enter_context(nc.sbuf_tensor("ident", [128, 128], BF16))
        ones1 = P.stack.enter_context(nc.sbuf_tensor("ones1", [1, 128], BF16))
        cc = P.stack.enter_context(nc.sbuf_tensor("cc", [128, 8], F32))
        identf = P.stack.enter_context(
            nc.sbuf_tensor("identf", [128, 128], F32))
        env.sb["identf"] = identf
        onescol = P.stack.enter_context(
            nc.sbuf_tensor("onescol", [128, 1], BF16))
        onesrow = P.stack.enter_context(
            nc.sbuf_tensor("onesrow", [1, 512], BF16))
        env.sb["onescol"] = onescol
        env.sb["onesrow"] = onesrow
        env.sb["res"] = res
        env.sb["ident"] = ident
        env.sb["ones1"] = ones1
        env.sb["cc"] = cc

        xin = aps["xres"].rearrange("s (i p) d -> p (s i) d", p=128)
        rest = res[:].rearrange("p (i d) -> p i d", d=1024)
        dma(P, "gpsimd", rest, xin)
        dma(P, "gpsimd", ident[:], aps["C_identb"])
        dma(P, "gpsimd", identf[:], aps["C_identf"])
        ve(P, "memset", onescol[:], 1.0)
        ve(P, "memset", onesrow[:], 1.0)
        dma(P, "gpsimd", ones1[:], aps["C_ones1"])
        P.barrier()

        for s in subs:
            SUBLAYERS[s][1](P, env)

        P.barrier()
        xo = xout.rearrange("s (i p) d -> p (s i) d", p=128)
        st = dma(P, "gpsimd", xo, rest)
        P.emit("gpsimd", None,
               waits=[(s, P.cur(s)) for s in P.cnt])

        with nc.Block() as block:
            P.replay(block)
    return nc


def _run_sim(nc, in_map):
    from concourse.bass_interp import CoreSim
    sim = CoreSim(nc, require_finite=False, require_nnan=False)
    for k, v in in_map.items():
        view = sim.tensor(k)
        view[:] = v
    sim.simulate()
    return {"xout": np.array(sim.tensor("xout"))}


def run_chain(inputs, groups, x0=None, n_cores=N_CORES, sim=False):
    consts = host_prep(inputs, [s for g in groups for s in g])
    allmaps = {}
    for k, v in inputs.items():
        v = np.atleast_1d(np.asarray(v))
        allmaps[k] = _f32(v) if v.dtype in (np.float32, np.float64) else v
    allmaps.update(consts)

    x = _f32(inputs["L0_attn_x"]) if x0 is None else _f32(x0)
    xs = [np.ascontiguousarray(x[2 * c:2 * c + 2]) for c in range(n_cores)]

    for group in groups:
        shapes = {}
        for k, v in allmaps.items():
            v = np.asarray(v)
            dt = BF16 if v.dtype == BF else F32
            shapes[k] = (v.shape, dt)
        shapes["xres"] = ((SEQS, T, D), F32)
        nc = build_group(group, shapes)
        in_maps = []
        for c in range(n_cores):
            m = dict(allmaps)
            m["xres"] = xs[c]
            in_maps.append(m)
        if sim:
            xs = [_run_sim(nc, in_maps[c]) for c in range(n_cores)]
            xs = [x["xout"] for x in xs]
        else:
            r = run_bass_kernel_spmd(nc, in_maps,
                                     core_ids=list(range(n_cores)))
            xs = [rr["xout"] for rr in r.results]
    return np.concatenate(xs, axis=0)


def kernel(**inputs):
    groups = [[s for s in CHAIN if s in SUBLAYERS]]
    return run_chain(inputs, groups)
